# version 11
# baseline (speedup 1.0000x reference)
import math
import numpy as np

import concourse.bass as bass
import concourse.mybir as mybir
from concourse.bass_utils import run_bass_kernel_spmd

F32 = mybir.dt.float32
BF16 = mybir.dt.bfloat16
I32 = mybir.dt.int32
AF = mybir.ActivationFunctionType
ALU = mybir.AluOpType

B = 8
S = 2048
D = 1024
NCH = D // 128
HD = 64
H = 16
KVH = 4
LAT = 256
DI = 2730
DIP = 2816
NH2 = DIP // 128
DEPTH = 6
XP = 2052

PAIRS = [(0, 4), (1, 5), (2, 6), (3, 7), (8, 12), (9, 13), (10, 14), (11, 15)]

def _split(v):
    hi = np.float32(v)
    lo = np.float32(v - np.float64(hi))
    return float(hi), float(lo)

PI_HI, PI_LO = _split(math.pi)
IPI_HI, IPI_LO = _split(1.0 / math.pi)


ENG_SEM = {"vector": "DV", "scalar": "AC", "gpsimd": "GP"}


class Emit:
    ENGS = ("sync", "tensor", "vector", "scalar", "gpsimd")

    def __init__(self):
        self.ops = {e: [] for e in self.ENGS}
        self.cnt = {}
        self.sems = {}
        self.mark = {}

    def sem(self, name):
        self.cnt[name] = 0

    def op(self, eng, fn, wait=(), inc=None, chain=True):
        wait = [(s, v) for (s, v) in wait if v > 0]
        es = ENG_SEM.get(eng)
        if es is not None:
            if chain and self.cnt[es] > 0:
                wait = list(wait) + [(es, self.cnt[es])]
            assert inc is None
            inc = (es, 1)
        w = []
        for sname, v in wait:
            key = (eng, sname)
            if v > self.mark.get(key, 0):
                self.mark[key] = v
                w.append((sname, v))
        ret = None
        if inc is not None:
            self.cnt[inc[0]] += inc[1]
            ret = self.cnt[inc[0]]
        self.ops[eng].append((fn, tuple(w), inc))
        return ret

    def dma(self, eng, out_ap, in_ap, sem, wait=()):
        wait = [(s, v) for (s, v) in wait if v > 0]
        w = []
        for sname, v in wait:
            key = (eng, sname)
            if v > self.mark.get(key, 0):
                self.mark[key] = v
                w.append((sname, v))
        self.cnt[sem] += 16
        self.ops[eng].append((
            lambda e, o=out_ap, i=in_ap: e.dma_start(out=o, in_=i),
            tuple(w), (sem, 16)))
        return self.cnt[sem]

    def build(self, block):
        sems = self.sems

        def mk(eng):
            items = self.ops[eng]

            def body(e):
                for fn, waits, inc in items:
                    for sname, v in waits[:-1]:
                        e.wait_ge(sems[sname], v)
                    inst = fn(e)
                    if waits:
                        inst._wait_ge(sems[waits[-1][0]], waits[-1][1])
                    if inc is not None:
                        inst.then_inc(sems[inc[0]], inc[1])
            return body
        block.sync(mk("sync"))
        block.tensor(mk("tensor"))
        block.vector(mk("vector"))
        block.scalar(mk("scalar"))
        block.gpsimd(mk("gpsimd"))


def host_consts():
    c = {}
    t32 = np.arange(S, dtype=np.float32)
    i32 = np.arange(0, HD, 2, dtype=np.float32)
    didx = np.tile(np.arange(32), 4)
    sgn = np.ones((128, 1), np.float32)
    for qq in range(4):
        if (qq % 2) == 0:
            sgn[qq * 32:(qq + 1) * 32] = -1.0
    tabs = np.zeros((128, 4 * S), np.float32)
    for sl, base in ((0, 1.0e6), (2, 1.0e4)):
        invf = (np.float32(1.0) / (np.float32(base) ** (i32 / np.float32(HD)))
                ).astype(np.float32)
        fr = (t32[None, :] * invf[didx][:, None]).astype(np.float32)
        tabs[:, sl * S:(sl + 1) * S] = np.cos(fr)
        tabs[:, (sl + 1) * S:(sl + 2) * S] = np.sin(fr) * sgn
    c["c_tabs"] = tabs
    xp = np.zeros((128, 2 * S), np.float32)
    bs = ((i32 + np.float32(0.4 * HD)) / np.float32(1.4 * HD)).astype(np.float32)
    pw = ((t32 - np.float32(S // 2)) / np.float32(128)).astype(np.float32)
    scl = (bs[didx][:, None] ** pw[None, :]).astype(np.float32)
    xp[:, 0:S] = scl
    xp[:, S:2 * S] = (np.float32(1.0) / scl).astype(np.float32)
    c["c_xpos"] = xp
    m = np.zeros((128, 128), np.float32)
    m[:, 0] = 1.0
    c["c_sumsel"] = m
    band = np.zeros((128, 144), np.float32)
    band[:64, 14] = 1.0
    band[64:, 15] = 1.0
    c["c_band"] = band
    band2 = np.zeros((128, 144), np.float32)
    band2[:64, 11] = 1.0
    band2[64:, 15] = 1.0
    c["c_band_mla"] = band2
    def sel(f):
        mm = np.zeros((128, 128), np.float32)
        for j in range(128):
            mm[f(j), j] = 1.0
        return mm
    c["c_sel_pair"] = sel(lambda j: 64 if j < 64 else 96)
    sel_mla = np.zeros((8, 128, 128), np.float32)
    for cc in range(8):
        hA, hB = PAIRS[cc]
        sel_mla[cc] = sel(lambda j, hA=hA, hB=hB: hA if j < 64 else hB)

    sel_ffn = np.zeros((8, 128, 128), np.float32)
    for cc in range(8):
        sel_ffn[cc] = sel(lambda j, cc=cc: 2 * cc + (j >= 64))

    sel_kv = np.zeros((2, 128, 128), np.float32)
    for mm_ in range(2):
        sel_kv[mm_] = sel(lambda j, m=mm_: 2 * m + (j >= 64))
    selc = np.zeros((16, 16 * 128), np.float32)
    for cc in range(8):
        selc[:, cc * 128:(cc + 1) * 128] = sel_mla[cc][0:16]
        selc[:, (8 + cc) * 128:(9 + cc) * 128] = sel_ffn[cc][0:16]
    c["c_selc"] = selc
    c["c_ident"] = np.eye(128, dtype=np.float32)
    kr = np.arange(128)[:, None]
    jE = np.arange(896)[None, :]
    c["c_maskE"] = (jE >= kr + 384).astype(np.float32)
    jE2 = np.arange(400)[None, :]
    c["c_maskE2"] = (jE2 <= kr + 128).astype(np.float32)
    return c


def blk_lhsT(W, perm_cols=None):
    K, M = W.shape
    if perm_cols is not None:
        W = W[:, perm_cols]
    nk, nm = K // 128, M // 128
    return np.ascontiguousarray(
        W.reshape(nk, 128, nm, 128).transpose(2, 1, 0, 3).reshape(nm, 128, nk * 128))


def perm_mla_cols():
    perm64 = list(range(0, 64, 2)) + list(range(1, 64, 2))
    cols = []
    for hA, hB in PAIRS:
        for h in (hA, hB):
            cols.extend(64 * h + p for p in perm64)
    return np.array(cols)


def perm_kv_cols():
    perm64 = list(range(0, 64, 2)) + list(range(1, 64, 2))
    cols = []
    for kv in range(4):
        cols.extend(64 * kv + p for p in perm64)
    return np.array(cols)


def mla_row_perm():
    rows = []
    for hA, hB in PAIRS:
        for h in (hA, hB):
            rows.extend(range(64 * h, 64 * h + 64))
    return np.array(rows)


def host_weights(inp):
    w = {}
    pq = perm_mla_cols()
    pk = perm_kv_cols()
    pr = mla_row_perm()
    w["g_wq"] = blk_lhsT(np.asarray(inp["L0_attn_g_wq"]), pq)
    w["g_kvd"] = blk_lhsT(np.asarray(inp["L0_attn_g_kv_down"]))
    w["g_kup"] = blk_lhsT(np.asarray(inp["L0_attn_g_k_up"]), pk)
    w["g_vup"] = np.ascontiguousarray(
        np.asarray(inp["L0_attn_g_v_up"]).reshape(2, 128, 256))
    w["g_wo"] = blk_lhsT(np.asarray(inp["L0_attn_g_wo"])[pr])
    w["l_wq"] = blk_lhsT(np.asarray(inp["L0_attn_l_wq"]), pq)
    w["l_wk"] = blk_lhsT(np.asarray(inp["L0_attn_l_wk"]), pk)
    w["l_wv"] = np.ascontiguousarray(
        np.asarray(inp["L0_attn_l_wv"]).reshape(NCH, 128, 256))
    w["l_wo"] = blk_lhsT(np.asarray(inp["L0_attn_l_wo"])[pr])
    n1 = np.asarray(inp["L0_attn_norm1_w"])
    n2 = np.asarray(inp["L0_attn_norm2_w"])
    w["normcols"] = np.ascontiguousarray(
        np.concatenate([n1.reshape(NCH, 128).T, n2.reshape(NCH, 128).T], axis=1))
    Wqkv = np.asarray(inp["L0_ffn_Wqkv"])
    w["f_wq"] = np.stack([blk_lhsT(Wqkv[i, :, 0:D]) for i in range(DEPTH)])
    w["f_wk"] = np.stack([blk_lhsT(Wqkv[i, :, D:2 * D]) for i in range(DEPTH)])
    w["f_wv"] = np.stack([np.ascontiguousarray(
        Wqkv[i, :, 2 * D:3 * D].reshape(NCH, 128, D)) for i in range(DEPTH)])
    Wo = np.asarray(inp["L0_ffn_Wo"])
    w["f_wo"] = np.stack([blk_lhsT(Wo[i]) for i in range(DEPTH)])
    W1 = np.asarray(inp["L0_ffn_W1"])
    W2 = np.asarray(inp["L0_ffn_W2"])
    b1 = np.asarray(inp["L0_ffn_b1"])
    w1blk = np.zeros((DEPTH, NH2, 128, 2048), np.float32)
    w2blk = np.zeros((DEPTH, 8, 128, DIP), np.float32)
    b1blk = np.zeros((DEPTH, 128, 2 * NH2), np.float32)
    for i in range(DEPTH):
        W1a = np.zeros((D, DIP), np.float32); W1a[:, :DI] = W1[i, :, :DI]
        W1g = np.zeros((D, DIP), np.float32); W1g[:, :DI] = W1[i, :, DI:]
        a_ = blk_lhsT(W1a)
        g_ = blk_lhsT(W1g)
        w1blk[i, :, :, 0:1024] = a_
        w1blk[i, :, :, 1024:2048] = g_
        W2p = np.zeros((DIP, D), np.float32); W2p[:DI] = W2[i]
        w2blk[i] = blk_lhsT(W2p)
        b1a = np.zeros((DIP,), np.float32); b1a[:DI] = b1[i, :DI]
        b1g = np.zeros((DIP,), np.float32); b1g[:DI] = b1[i, DI:]
        b1blk[i, :, 0:NH2] = b1a.reshape(NH2, 128).T
        b1blk[i, :, NH2:] = b1g.reshape(NH2, 128).T
    w["f_w1"] = w1blk
    w["f_w2"] = w2blk
    w["f_b1"] = b1blk
    w["f_b2"] = np.ascontiguousarray(
        np.asarray(inp["L0_ffn_b2"]).reshape(DEPTH, 8, 128).transpose(2, 0, 1).reshape(128, DEPTH * 8))
    ln_g = np.asarray(inp["L0_ffn_ln_g"])
    ln_b = np.asarray(inp["L0_ffn_ln_b"])
    lncol = np.zeros((DEPTH, 128, 16), np.float32)
    for L in range(DEPTH):
        lncol[L, :, 0:8] = ln_g[L].reshape(NCH, 128).T
        lncol[L, :, 8:16] = ln_b[L].reshape(NCH, 128).T
    w["f_lnrow"] = lncol
    qs = np.asarray(inp["L0_ffn_q_scale"])
    ks = np.asarray(inp["L0_ffn_k_scale"])
    w["f_qscol"] = np.ascontiguousarray(np.tile(qs.T, (2, 1)))
    w["f_kscol"] = np.ascontiguousarray(np.tile(ks.T, (2, 1)))
    qsr = np.zeros((128, 2 * DEPTH * HD), np.float32)
    qsr[0, :DEPTH * HD] = qs.reshape(-1)
    qsr[0, DEPTH * HD:] = ks.reshape(-1)
    w["f_qksrow"] = qsr
    return w


def build_program(upto=None):
    nc = bass.Bass("TRN2", target_bir_lowering=False, debug=False)
    em = Emit()
    for s in ("LD", "LDV", "LDL", "LDX", "SL", "GPD", "GPD2", "GPC",
              "WS", "WS2", "PE", "DV", "AC", "GP", "SP", "ST", "XSP"):
        em.sem(s)

    dr = {}
    dr["xT"] = nc.dram_tensor("xT", [D, S], F32, kind="ExternalInput").ap()
    dr["out"] = nc.dram_tensor("out", [D, S], F32, kind="ExternalOutput").ap()
    dr["x3sp"] = nc.dram_tensor("x3sp", [D, S], F32, kind="Internal").ap()
    for name, shape in [
        ("c_tabs", [128, 4 * S]), ("c_xpos", [128, 2 * S]),
        ("c_sumsel", [128, 128]), ("c_band", [128, 144]), ("c_band_mla", [128, 144]),
        ("c_sel_pair", [128, 128]), ("c_selc", [16, 16 * 128]),
        ("c_ident", [128, 128]),
        ("c_maskE", [128, 896]), ("c_maskE2", [128, 400]),
        ("g_wq", [8, 128, 1024]), ("g_kvd", [2, 128, 1024]),
        ("g_kup", [2, 128, 256]), ("g_vup", [2, 128, 256]),
        ("g_wo", [8, 128, 1024]),
        ("l_wq", [8, 128, 1024]), ("l_wk", [2, 128, 1024]),
        ("l_wv", [NCH, 128, 256]), ("l_wo", [8, 128, 1024]),
        ("normcols", [128, 2 * NCH]),
        ("f_wq", [DEPTH, 8, 128, 1024]), ("f_wk", [DEPTH, 8, 128, 1024]),
        ("f_wv", [DEPTH, NCH, 128, D]), ("f_wo", [DEPTH, 8, 128, 1024]),
        ("f_w1", [DEPTH, NH2, 128, 2048]), ("f_w2", [DEPTH, 8, 128, DIP]),
        ("f_b1", [DEPTH, 128, 2 * NH2]), ("f_b2", [128, DEPTH * 8]),
        ("f_lnrow", [DEPTH, 128, 16]), ("f_qscol", [128, DEPTH]),
        ("f_kscol", [128, DEPTH]), ("f_qksrow", [128, 2 * DEPTH * HD]),
    ]:
        dr[name] = nc.dram_tensor(name, shape, F32, kind="ExternalInput").ap()

    import contextlib
    stack = contextlib.ExitStack()
    sb = {}

    def alloc(name, shape, dtype=F32):
        sb[name] = stack.enter_context(nc.sbuf_tensor(name, shape, dtype))
        return sb[name]

    with stack:
        x = alloc("x", [128, NCH * XP])
        tabs = alloc("tabs", [128, 4 * S])
        wb = alloc("wb", [128, 2 * 1024])
        csm = alloc("csm", [128, 64])
        b1c = alloc("b1c", [128, 2 * NH2])
        b2c = alloc("b2c", [128, DEPTH * 8])
        qkc = alloc("qkc", [128, 2 * DEPTH])
        seli = alloc("seli", [128, 2112])
        selc = alloc("selc", [16, 16 * 128])
        ARW = 22208
        arf_h = nc.virtual_sbuf_tensor("arf", [128, ARW], F32)
        arb_h = nc.virtual_sbuf_tensor("arbv", [128, 2 * ARW], BF16)
        aru = stack.enter_context(nc.sbuf_tensor("aru", [128, 4 * ARW],
                                                 mybir.dt.uint8))
        for _h in (arf_h, arb_h):
            nc._tensor(_h.name, _h.shape, _h.dtype, type="SB")
            _ml = nc.lookup_mloc(_h)
            _ml.allocated = not nc.target_bir_lowering
            _ml.addr = nc.lookup_mloc(aru).addr
        ar = arf_h.ap()
        arB = arb_h.ap()
        pb = [stack.enter_context(nc.psum_tensor(f"pb{i}", [128, 512], F32))
              for i in range(8)]

        class Arena:
            def __init__(self, base_ap, size):
                self.ap = base_ap
                self.size = size
                self.off = 0
                self.marks = []

            def take(self, n):
                n = (n + 15) // 16 * 16
                assert self.off + n <= self.size, (self.off, n, self.size)
                v = self.ap[:, self.off:self.off + n]
                self.off += n
                return v

            def push(self):
                self.marks.append(self.off)

            def pop(self):
                self.off = self.marks.pop()
        A = Arena(ar, ARW)

        def takeB(nb):
            off0 = A.off
            A.take((nb + 1) // 2)
            return arB[:, 2 * off0:2 * off0 + nb]

        col_eps6 = csm[:, 0:1]
        col_eps5 = csm[:, 1:2]
        col_neg8 = csm[:, 2:3]
        col_negc = csm[:, 3:4]
        n1col = csm[:, 8:16]
        n2col = csm[:, 16:24]

        def dma(eng, out_ap, in_ap, wait=(), sem="LD"):
            return em.dma(eng, out_ap, in_ap, sem, wait=wait)

        for c in range(NCH):
            dma("sync", x[:, c * XP + 1:c * XP + 1 + S], dr["xT"][c * 128:(c + 1) * 128, :])
        dma("sync", csm[:, 8:24], dr["normcols"][:, :])
        dma("sync", b2c[:, :], dr["f_b2"][:, :])
        dma("sync", qkc[:, 0:DEPTH], dr["f_qscol"][:, :])
        dma("sync", qkc[:, DEPTH:2 * DEPTH], dr["f_kscol"][:, :])
        dma("sync", seli[:, 0:128], dr["c_sumsel"][:, :])
        dma("sync", seli[:, 128:256], dr["c_sel_pair"][:, :])
        dma("sync", seli[:, 384:528], dr["c_band"][:, :])
        dma("sync", seli[:, 528:672], dr["c_band_mla"][:, :])
        dma("sync", seli[:, 672:1568], dr["c_maskE"][:, :])
        dma("sync", seli[:, 1568:1968], dr["c_maskE2"][:, :])
        dma("sync", seli[:, 1968:2096], dr["c_ident"][:, :])
        dma("sync", selc[:, :], dr["c_selc"][:, :])
        sumsel = seli[:, 0:128]
        identm = seli[:, 1968:2096]
        selpair = seli[:, 128:256]

        def band_ffn(c):
            return seli[:, 384 + 14 - 2 * c:384 + 142 - 2 * c]

        def band_mla(c):
            o = (11 - c) if c < 4 else (11 - c - 4)
            return seli[:, 528 + o:528 + o + 128]

        def pen_ge(delta):
            return 672 + 384 - delta

        def pen_le(delta):
            return 1568 + 128 - delta

        for c in range(NCH):
            em.op("gpsimd", lambda e, c=c: e.memset(x[:, c * XP:c * XP + 1], 0.0))
        em.op("gpsimd", lambda e: e.memset(col_eps6, 1e-6))
        em.op("gpsimd", lambda e: e.memset(col_eps5, 1e-5))
        em.op("gpsimd", lambda e: e.memset(col_neg8, -8.0))
        gp_init = em.cnt["GP"]

        dma("sync", tabs[:, :], dr["c_tabs"][:, :])
        ld_tab = ld_init = em.cnt["LD"]
        tabs_dv = 0
        tabs_pe = 0
        tabs_ac = 0

        def xsl(c, t0, w, shift=False):
            o = c * XP + 1 + t0 - (1 if shift else 0)
            return x[:, o:o + w]

        def finish():
            dv = em.cnt["DV"]
            for c in range(NCH):
                em.dma("gpsimd", dr["out"][c * 128:(c + 1) * 128, :],
                       xsl(c, 0, S), "ST", wait=[("DV", dv), ("PE", em.cnt["PE"]),
                                                 ("AC", em.cnt["AC"])])
            em.op("gpsimd", lambda e: e.nop(), wait=[("ST", em.cnt["ST"])])
            for name_ in em.cnt:
                em.sems[name_] = stack.enter_context(nc.semaphore(name_))
            blk = stack.enter_context(nc.Block())
            em.build(blk)

        bankdep = {}

        def bwait(i):
            d = bankdep.get(i)
            return [d] if d else []

        def pemm(out_ap, lhsT, rhs, start, stop, wait=(), inc=False):
            em.op("tensor", lambda e, o=out_ap, l=lhsT, r=rhs, s1=start, s2=stop:
                  e.matmul(o, l, r, start=s1, stop=s2), wait=wait,
                  inc=("PE", 1) if (inc or stop) else None)
            return em.cnt["PE"]

        onesc = csm[0:1, 5:6]
        em.op("gpsimd", lambda e: e.memset(csm[:, 5:6], 1.0))
        onesrow = seli[0:1, 256:384]
        em.op("gpsimd", lambda e: e.memset(seli[0:1, 256:384], 1.0))
        gp_init2 = em.cnt["GP"]

        wsc = {"n": 0, "dep": {0: None, 1: None}}

        def wload(dram_blob, idx, ncols, scale_col=None, extra_wait=()):
            par = wsc["n"] % 2
            wsc["n"] += 1
            dst = wb[:, par * 1024:par * 1024 + ncols]
            sem = "WS" if par == 0 else "WS2"
            w = list(extra_wait)
            if wsc["dep"][par] is not None:
                w.append(wsc["dep"][par])
            wv = em.dma("sync", dst, dram_blob[idx], sem, wait=w)
            wsc["par"] = par
            if scale_col is None:
                return dst, [(sem, wv)], par
            dv = None
            for k in range(ncols // 128):
                dv = em.op("vector", lambda e, k=k, dst=dst: e.tensor_scalar(
                    out=dst[:, k * 128:(k + 1) * 128],
                    in0=dst[:, k * 128:(k + 1) * 128],
                    scalar1=scale_col[:, k:k + 1], scalar2=None, op0=ALU.mult),
                    wait=[(sem, wv)] if k == 0 else ())
            return dst, [("DV", dv)], par

        def wfree(par):
            wsc["dep"][par] = ("PE", em.cnt["PE"])

        wsc3 = {"dep": {0: None, 1: None}}

        def wload3(dram_blob, idx, ncols):
            par = wsc["n"] % 2
            wsc["n"] += 1
            dst = wb[:, par * 1024:par * 1024 + ncols]
            sem = "WS" if par == 0 else "WS2"
            w = []
            if wsc["dep"][par] is not None:
                w.append(wsc["dep"][par])
            wv = em.dma("sync", dst, dram_blob[idx], sem, wait=w)
            hh = wb3h[:, par * 1024:par * 1024 + ncols]
            ll = wb3l[:, par * 1024:par * 1024 + ncols]
            w2 = []
            if wsc3["dep"][par] is not None:
                w2.append(wsc3["dep"][par])
            gv = em.op("gpsimd", lambda e, hh=hh, dst=dst:
                       e.tensor_copy(hh, dst),
                       wait=[(sem, wv)] + w2)
            dvc = em.op("vector", lambda e, hh=hh, ll=ll, dst=dst:
                        e.scalar_tensor_tensor(out=ll, in0=hh, scalar=-1.0,
                                               in1=dst, op0=ALU.mult,
                                               op1=ALU.add),
                        wait=[("GP", gv)])
            wsc["dep"][par] = ("DV", dvc)
            return hh, ll, [("DV", dvc), ("GP", gv)], par

        def wfree3(par):
            wsc3["dep"][par] = ("PE", em.cnt["PE"])

        def mm3(bank, wh, wl, rh, rl, k, nk, w0):
            pemm(bank, wh, rh, k == 0, False, wait=w0 if k == 0 else (),
                 inc=True)
            pemm(bank, wh, rl, False, False, inc=True)
            return pemm(bank, wl, rh, False, k == nk - 1, inc=True)

        def emit_rms(pg, xdep):
            rows, sq = pg["rows"], pg["sq"]
            mmlast = {}
            for tt in range(4):
                t0 = tt * 512
                for c in range(NCH):
                    sl = sq[:, (c % 2) * 512:(c % 2) * 512 + 512]
                    acv = em.op("scalar", lambda e, c=c, t0=t0, sl=sl: e.activation(
                        out=sl, in_=xsl(c, t0, 512), func=AF.Square),
                        wait=list(xdep) + ([("PE", mmlast[c % 2])] if c % 2 in mmlast
                                           else []))
                    mmlast[c % 2] = pemm(pb[2][:], sumsel, sl, c == 0, c == NCH - 1,
                                         wait=[("AC", acv)] + (bwait(2) if c == 0
                                                               else []), inc=True)
                pev = mmlast[(NCH - 1) % 2]
                a2 = em.op("scalar", lambda e, t0=t0: e.activation(
                    out=rows[0:1, t0:t0 + 512], in_=pb[2][0:1, :], func=AF.Sqrt,
                    scale=1.0 / D, bias=col_eps6[0:1, 0:1]), wait=[("PE", pev)])
                bankdep[2] = ("AC", a2)
                em.op("vector", lambda e, t0=t0: e.reciprocal(
                    out=rows[0:1, t0:t0 + 512], in_=rows[0:1, t0:t0 + 512]),
                    wait=[("AC", a2)])
            dvr = em.cnt["DV"]
            for tc in range(16):
                pemm(pb[3][:, tc:tc + 1], rows[0:1, tc * 128:(tc + 1) * 128],
                     onesc, True, True,
                     wait=([("DV", dvr), ("GP", gp_init2)] + bwait(3)) if tc == 0
                     else [], inc=(tc == 15))
            pg["rscoldep"] = em.op("vector", lambda e: e.tensor_copy(
                pg["rscol"], pb[3][:, 0:16]), wait=[("PE", em.cnt["PE"])])
            bankdep[3] = ("DV", pg["rscoldep"])

        def emit_rs16(pg, qt0, w):
            spv = None
            for r in range(16):
                spv = em.dma("gpsimd", pg["rs16"][r:r + 1, 0:w],
                             pg["rows"][0:1, qt0:qt0 + w], "SP",
                             wait=[("DV", em.cnt["DV"])] if r == 0 else ())
            pg["rs16dep"] = em.op("vector", lambda e, w=w: e.tensor_tensor(
                out=pg["rs16sq"][0:16, 0:w], in0=pg["rs16"][0:16, 0:w],
                in1=pg["rs16"][0:16, 0:w], op=ALU.mult), wait=[("SP", spv)])

        def emit_gqa(cfg):
            name = cfg["name"]
            Ct, St, win = cfg["Ct"], cfg["St"], cfg["window"]
            QT = cfg["QT"]
            nqt = S // QT
            A.push()
            pg = {}
            pg["rows"] = A.take(S)
            pg["rs16"] = A.take(512)
            pg["rs16sq"] = A.take(512)
            pg["rscol"] = A.take(16)
            pg["sq"] = A.take(1024)
            pg["rs16dep"] = 0
            kpr = A.take(2 * S)
            vsb = A.take(16 * 260)
            qu = A.take(NCH * QT)
            ao = A.take(NCH * QT)
            rn16 = A.take(512)
            lat = qu[:, 0:1024] if cfg["lat"] else None
            pring = A.take(4 * QT)
            scr = pg["sq"]

            emit_rms(pg, cfg["xdep"])
            gpz = em.op("gpsimd", lambda e: e.memset(rn16, 0.0),
                        wait=[("DV", em.cnt["DV"]), ("PE", em.cnt["PE"]),
                              ("AC", em.cnt["AC"])])
            for tc in range(16):
                em.op("gpsimd", lambda e, tc=tc: e.memset(
                    vsb[:, tc * 260 + 64:(tc + 1) * 260:65], 1.0))
            gp_ones = em.cnt["GP"]

            def rope_qknorm(mcs, t0, w, bandfn, selbase, nrows, banks, pe_done,
                            dsts, extra_dst_wait):
                for i, mc in enumerate(mcs):
                    p = banks[i]
                    em.op("vector", lambda e, p=p, d=dsts[i], t0=t0, w=w:
                          e.tensor_tensor(out=d, in0=p[:, 0:w],
                                          in1=tabs[:, Ct * S + t0:Ct * S + t0 + w],
                                          op=ALU.mult),
                          wait=[("PE", pe_done[i])] + (extra_dst_wait[i] or []))
                    for qd in range(4):
                        pq, ps = qd * 32, (qd ^ 1) * 32
                        em.op("vector", lambda e, p=p, pq=pq, ps=ps, t0=t0, w=w:
                              e.tensor_tensor(
                                  out=scr[pq:pq + 32, 0:w], in0=p[ps:ps + 32, 0:w],
                                  in1=tabs[pq:pq + 32, St * S + t0:St * S + t0 + w],
                                  op=ALU.mult))
                    dvu = em.op("vector", lambda e, d=dsts[i], w=w: e.tensor_tensor(
                        out=d, in0=d, in1=scr[:, 0:w], op=ALU.add))
                    acs = em.op("scalar", lambda e, d=dsts[i], w=w: e.activation(
                        out=scr[:, 512:512 + w], in_=d, func=AF.Square),
                        wait=[("DV", dvu)] + ([("PE", em.cnt["PE"])] if i > 0 else
                                              bwait(4)))
                    pemm(pb[4][:, 0:w], bandfn(mc), scr[:, 512:512 + w],
                         i == 0, i == len(mcs) - 1, wait=[("AC", acs)], inc=True)
                pev = em.cnt["PE"]
                d1 = em.op("vector", lambda e, w=w, n=nrows: e.tensor_tensor(
                    out=rn16[0:n, 0:w], in0=pb[4][0:n, 0:w],
                    in1=pg["rs16sq"][0:n, 0:w], op=ALU.mult),
                    wait=[("PE", pev), ("GP", gpz), ("DV", pg["rs16dep"])])
                bankdep[4] = ("DV", d1)
                a1 = em.op("scalar", lambda e, w=w, n=nrows: e.activation(
                    out=rn16[0:n, 0:w], in_=rn16[0:n, 0:w], func=AF.Sqrt,
                    scale=1.0 / HD, bias=col_eps6[0:n, 0:1]), wait=[("DV", d1)])
                em.op("vector", lambda e, w=w, n=nrows: e.reciprocal(
                    out=rn16[0:n, 0:w], in_=rn16[0:n, 0:w]), wait=[("AC", a1)])
                d3 = em.op("vector", lambda e, w=w, n=nrows: e.tensor_tensor(
                    out=rn16[0:n, 0:w], in0=rn16[0:n, 0:w],
                    in1=pg["rs16"][0:n, 0:w], op=ALU.mult))
                out_dv = []
                for i, mc in enumerate(mcs):
                    pevb = pemm(pb[5][:, 0:w],
                                selc[0:nrows, (selbase + mc) * 128:
                                     (selbase + mc) * 128 + 128],
                                rn16[0:nrows, 0:w], True, True,
                                wait=[("DV", d3)] + bwait(5))
                    dvm = em.op("vector", lambda e, d=dsts[i], w=w: e.tensor_tensor(
                        out=d, in0=d, in1=pb[5][:, 0:w], op=ALU.mult),
                        wait=[("PE", pevb)])
                    bankdep[5] = ("DV", dvm)
                    out_dv.append(dvm)
                return out_dv

            for tt in range(4):
                t0 = tt * 512
                emit_rs16(pg, t0, 512)
                pe_done = []
                if cfg["lat"]:
                    latdv = []
                    for mc in range(2):
                        waps, wwait, wpar = wload(dr["g_kvd"], mc, 1024,
                                                  scale_col=cfg["ncol"])
                        for k in range(NCH):
                            pemm(pb[mc][:], waps[:, k * 128:(k + 1) * 128],
                                 xsl(k, t0, 512), k == 0, k == NCH - 1,
                                 wait=(wwait + bwait(mc)) if k == 0 else ())
                        wfree(wpar)
                        pev = em.cnt["PE"]
                        latdv.append(em.op("vector", lambda e, mc=mc: e.tensor_copy(
                            lat[:, mc * 512:(mc + 1) * 512], pb[mc][:]),
                            wait=[("PE", pev)]))
                        bankdep[mc] = ("DV", latdv[-1])
                    for mc in range(2):
                        waps, wwait, wpar = wload(dr["g_kup"], mc, 256)
                        for kc in range(2):
                            pemm(pb[mc][:], waps[:, kc * 128:(kc + 1) * 128],
                                 lat[:, kc * 512:(kc + 1) * 512], kc == 0, kc == 1,
                                 wait=(wwait + bwait(mc) + [("DV", latdv[1])])
                                 if kc == 0 else ())
                        wfree(wpar)
                        pe_done.append(em.cnt["PE"])
                else:
                    for mc in range(2):
                        waps, wwait, wpar = wload(cfg["wk"], mc, 1024,
                                                  scale_col=cfg["ncol"])
                        for k in range(NCH):
                            pemm(pb[mc][:], waps[:, k * 128:(k + 1) * 128],
                                 xsl(k, t0, 512), k == 0, k == NCH - 1,
                                 wait=(wwait + bwait(mc)) if k == 0 else ())
                        wfree(wpar)
                        pe_done.append(em.cnt["PE"])
                dsts = [kpr[:, mc * S + t0:mc * S + t0 + 512] for mc in range(2)]
                kdvs = rope_qknorm([0, 1], t0, 512, band_ffn, 8, 4,
                                   [pb[0], pb[1]], pe_done, dsts,
                                   [cfg["kdst_wait"](tt, mc) for mc in range(2)])
                bankdep[0] = ("DV", kdvs[0])
                bankdep[1] = ("DV", kdvs[1])
                cfg["kdv"][tt] = kdvs
                for tc in range(4):
                    tca = tt * 4 + tc
                    if cfg["lat"]:
                        for kc in range(2):
                            pemm(pb[6][:, 0:256],
                                 lat[:, kc * 512 + tc * 128:kc * 512 + tc * 128 + 128],
                                 vup_ap[:, kc * 256:(kc + 1) * 256], kc == 0, kc == 1,
                                 wait=(bwait(6) + vup_wait + [("DV", latdv[1])])
                                 if kc == 0 else ())
                    else:
                        for kc in range(NCH):
                            pemm(pb[6][:, 0:256],
                                 xsl(kc, t0 + tc * 128, 128),
                                 lwv_ap[:, kc * 256:(kc + 1) * 256], kc == 0,
                                 kc == NCH - 1,
                                 wait=(bwait(6) + lwv_wait) if kc == 0 else ())
                    pev = em.cnt["PE"]
                    dvv = em.op("vector", lambda e, tca=tca: e.tensor_scalar(
                        out=vsb[:, tca * 260:tca * 260 + 260].rearrange(
                            "p (kv d) -> p kv d", kv=4, d=65)[:, :, 0:64],
                        in0=pb[6][:, 0:256].rearrange("p (kv d) -> p kv d",
                                                      kv=4, d=64),
                        scalar1=pg["rscol"][:, tca:tca + 1], scalar2=None,
                        op0=ALU.mult),
                        wait=[("PE", pev), ("GP", gp_ones),
                              ("DV", pg["rscoldep"])])
                    bankdep[6] = ("DV", dvv)
                    cfg["vdv"][tca] = dvv

            prev_attn_pe = 0
            pass1_dv = em.cnt["DV"]
            for qt in range(nqt):
                q0 = qt * QT
                emit_rs16(pg, q0, QT)
                qdv_byc = [0] * NCH
                for mp in range(NCH // 2):
                    mcs = [2 * mp, 2 * mp + 1]
                    pe_done = []
                    for i, mc in enumerate(mcs):
                        waps, wwait, wpar = wload(cfg["wq"], mc, 1024,
                                                  scale_col=cfg["ncol"])
                        for k in range(NCH):
                            pemm(pb[i][:, 0:QT], waps[:, k * 128:(k + 1) * 128],
                                 xsl(k, q0, QT), k == 0, k == NCH - 1,
                                 wait=(wwait + bwait(i)) if k == 0 else ())
                        wfree(wpar)
                        pe_done.append(em.cnt["PE"])
                    dsts = [qu[:, mc * QT:(mc + 1) * QT] for mc in mcs]
                    qdvs = rope_qknorm(mcs, q0, QT, cfg["bandq"], cfg["selq"], 16,
                                       [pb[0], pb[1]], pe_done, dsts,
                                       [[("PE", prev_attn_pe), ("DV", pass1_dv)]] * 2)
                    bankdep[0] = ("DV", qdvs[0])
                    bankdep[1] = ("DV", qdvs[1])
                    qdv_byc[mcs[0]] = qdvs[0]
                    qdv_byc[mcs[1]] = qdvs[1]
                kmaxc = (q0 + QT) // 128
                kminc = 0 if win is None else max(0, (q0 - win) // 128)
                nkc = kmaxc - kminc
                for pc in range(NCH):
                    hA, hB = cfg["pairs"][pc]
                    kvA, kvB = cfg["kvmap"](hA), cfg["kvmap"](hB)
                    pdep = {}
                    pvpe = {}
                    def emit_s(ki):
                        kc = kminc + ki
                        ri = ki % 2
                        delta = kc * 128 - q0
                        pcol = None
                        if delta > -128:
                            pcol = pen_ge(delta)
                        if win is not None and delta < QT - 1 - cfg["winhi"]:
                            pcol = pen_le(delta + cfg["winhi"])
                        for hi, kv in ((0, kvA), (1, kvB)):
                            base = 64 * hi
                            bank = pb[2 + hi]
                            sv = pemm(bank[:, 0:QT],
                                      kpr[base:base + 64,
                                          (kv // 2) * S + kc * 128:
                                          (kv // 2) * S + kc * 128 + 128],
                                      qu[base:base + 64, pc * QT:pc * QT + QT],
                                      True, True,
                                      wait=bwait(2 + hi) +
                                      ([("DV", qdv_byc[pc])] if ki == 0 else []),
                                      inc=True)
                            ring_w = ([("PE", pvpe[(ki - 2, hi)])]
                                      if (ki - 2, hi) in pvpe else [])
                            po = (ri * 2 + hi) * QT
                            acx = em.op("scalar", lambda e, bank=bank, po=po, QT=QT:
                                        e.activation(
                                out=pring[:, po:po + QT], in_=bank[:, 0:QT],
                                func=AF.Exp, bias=col_neg8[:, 0:1], scale=0.125),
                                wait=[("PE", sv)] + ring_w)
                            bankdep[2 + hi] = ("AC", acx)
                            if pcol is not None:
                                gpm = em.op("gpsimd", lambda e, po=po, QT=QT,
                                            pcol=pcol: e.tensor_tensor(
                                    out=pring[:, po:po + QT],
                                    in0=pring[:, po:po + QT],
                                    in1=seli[:, pcol:pcol + QT], op=ALU.mult),
                                    wait=[("AC", acx)])
                                pdep[(ki, hi)] = [("GP", gpm)]
                            else:
                                pdep[(ki, hi)] = [("AC", acx)]

                    def emit_pv(ki):
                        kc = kminc + ki
                        ri = ki % 2
                        for hi, kv in ((0, kvA), (1, kvB)):
                            po = (ri * 2 + hi) * QT
                            for half in range(2):
                                bnk = pb[4 + 2 * hi + half]
                                pvpe[(ki, hi)] = pemm(
                                    bnk[0:65, 0:QT],
                                    vsb[64 * half:64 * half + 64,
                                        kc * 260 + kv * 65:kc * 260 + kv * 65 + 65],
                                    pring[64 * half:64 * half + 64, po:po + QT],
                                    ki == 0, ki == nkc - 1,
                                    wait=pdep[(ki, hi)] +
                                    (bwait(4 + 2 * hi + half) if ki == 0 else []),
                                    inc=True)

                    for ki in range(nkc):
                        emit_s(ki)
                        if ki >= 1:
                            emit_pv(ki - 1)
                    emit_pv(nkc - 1)
                    for hi in range(2):
                        em.op("vector", lambda e, hi=hi, QT=QT: e.tensor_copy(
                            rn16[64 + 32 * hi:65 + 32 * hi, 0:QT],
                            pb[4 + 2 * hi][64:65, 0:QT]),
                            wait=[("PE", pvpe[(nkc - 1, hi)])])
                        em.op("vector", lambda e, hi=hi, QT=QT: e.tensor_tensor(
                            out=rn16[64 + 32 * hi:65 + 32 * hi, 0:QT],
                            in0=rn16[64 + 32 * hi:65 + 32 * hi, 0:QT],
                            in1=pb[5 + 2 * hi][64:65, 0:QT], op=ALU.add))
                        em.op("vector", lambda e, hi=hi, QT=QT: e.reciprocal(
                            out=rn16[64 + 32 * hi:65 + 32 * hi, 0:QT],
                            in_=rn16[64 + 32 * hi:65 + 32 * hi, 0:QT]))
                    dvr = em.cnt["DV"]
                    for hi in range(2):
                        em.op("vector", lambda e, hi=hi, pc=pc, QT=QT:
                              e.tensor_copy(
                            ao[64 * hi:64 * hi + 64, pc * QT:pc * QT + QT],
                            pb[4 + 2 * hi][0:64, 0:QT]))
                        em.op("vector", lambda e, hi=hi, pc=pc, QT=QT:
                              e.tensor_tensor(
                            out=ao[64 * hi:64 * hi + 64, pc * QT:pc * QT + QT],
                            in0=ao[64 * hi:64 * hi + 64, pc * QT:pc * QT + QT],
                            in1=pb[5 + 2 * hi][0:64, 0:QT], op=ALU.add))
                    pevb = pemm(pb[2][:, 0:QT], selpair, rn16[:, 0:QT], True, True,
                                wait=[("DV", dvr)] + bwait(2))
                    dvo = em.op("vector", lambda e, pc=pc, QT=QT: e.tensor_tensor(
                        out=ao[:, pc * QT:pc * QT + QT],
                        in0=ao[:, pc * QT:pc * QT + QT],
                        in1=pb[2][:, 0:QT], op=ALU.mult), wait=[("PE", pevb)])
                    bankdep[2] = ("DV", dvo)
                    for i_ in range(4, 8):
                        bankdep[i_] = ("DV", dvo)
                aodep = ("DV", em.cnt["DV"])
                for mc in range(NCH):
                    waps, wwait, wpar = wload(cfg["wo"], mc, 1024)
                    bk = mc % 2
                    for k in range(NCH):
                        pemm(pb[bk][:, 0:QT], waps[:, k * 128:(k + 1) * 128],
                             ao[:, k * QT:(k + 1) * QT], k == 0, k == NCH - 1,
                             wait=(wwait + bwait(bk) + [aodep])
                             if k == 0 else ())
                    wfree(wpar)
                    pev = em.cnt["PE"]
                    dvx = em.op("vector", lambda e, mc=mc, q0=q0, QT=QT, bk=bk:
                                e.tensor_tensor(
                        out=xsl(mc, q0, QT), in0=xsl(mc, q0, QT),
                        in1=pb[bk][:, 0:QT], op=ALU.add), wait=[("PE", pev)])
                    bankdep[bk] = ("DV", dvx)
                prev_attn_pe = em.cnt["PE"]
            A.pop()
            return em.cnt["DV"]

        A.push()
        vup_pg = A.take(512)
        vup_ap = vup_pg
        vw = em.dma("sync", vup_pg[:, 0:256], dr["g_vup"][0], "LDV",
                    wait=[("LD", ld_init)])
        vw = em.dma("sync", vup_pg[:, 256:512], dr["g_vup"][1], "LDV")
        vup_wait = [("LDV", vw)]
        lwv_ap = None
        lwv_wait = None

        mla_cfg = dict(name="mla", Ct=0, St=1, window=None, QT=256,
                       wq=dr["g_wq"], wo=dr["g_wo"], wk=None, ncol=n1col,
                       lat=True, kvmap=lambda h: h // 4, pairs=PAIRS,
                       bandq=band_mla, selq=0, winhi=None,
                       kdv={}, vdv={},
                       kdst_wait=lambda tt, mc: [],
                       xdep=[("LD", ld_init), ("GP", gp_init)])
        emit_gqa(mla_cfg)
        A.pop()

        if upto == "mla":
            finish()
            return nc, em

        A.push()
        lwv_pg = A.take(NCH * 256)
        lwv_ap = lwv_pg
        lwv_w = None
        for kc in range(NCH):
            lwv_w = em.dma("sync", lwv_pg[:, kc * 256:(kc + 1) * 256],
                           dr["l_wv"][kc], "LDL",
                           wait=[("DV", em.cnt["DV"]), ("AC", em.cnt["AC"]),
                                 ("GP", em.cnt["GP"])] if kc == 0 else ())
        lwv_wait = [("LDL", lwv_w)]
        lwv_dv = None
        for kc in range(NCH):
            lwv_dv = em.op("vector", lambda e, kc=kc: e.tensor_scalar(
                out=lwv_pg[:, kc * 256:(kc + 1) * 256],
                in0=lwv_pg[:, kc * 256:(kc + 1) * 256],
                scalar1=n2col[:, kc:kc + 1], scalar2=None, op0=ALU.mult),
                wait=lwv_wait if kc == 0 else ())
        lwv_wait = [("DV", lwv_dv)]
        xdv_mla = em.cnt["DV"]
        loc_cfg = dict(name="loc", Ct=2, St=3, window=256, QT=256,
                       wq=dr["l_wq"], wo=dr["l_wo"], wk=dr["l_wk"], ncol=n2col,
                       lat=False, kvmap=lambda h: h // 4, pairs=PAIRS,
                       bandq=band_mla, selq=0, winhi=255,
                       kdv={}, vdv={},
                       kdst_wait=lambda tt, mc: [],
                       xdep=[("DV", xdv_mla), ("PE", em.cnt["PE"])])
        emit_gqa(loc_cfg)
        A.pop()

        if upto == "local":
            finish()
            return nc, em

        def emit_ln(src2):
            A.push()
            rows = A.take(1536)
            pgx = A.take(4096)
            sq = A.take(1024)
            scrm = A.take(512)
            ldx0 = {"v": None}
            src2r = src2.rearrange("(c p) t -> p c t", p=128)
            for tt in range(4):
                t0 = tt * 512
                page = pgx
                pager = page.rearrange("p (c t) -> p c t", c=NCH)
                wv = em.dma("sync", pager, src2r[:, :, t0:t0 + 512], "LDX",
                            wait=[("DV", em.cnt["DV"])])
                mml = {}
                zdv = {}
                for c in range(NCH):
                    zdv[c] = em.op("vector", lambda e, c=c, t0=t0, page=page:
                                   e.tensor_tensor(
                        out=xsl(c, t0, 512), in0=xsl(c, t0, 512),
                        in1=page[:, c * 512:(c + 1) * 512], op=ALU.add),
                        wait=[("LDX", wv), ("PE", em.cnt["PE"])] if c == 0
                        else [("LDX", wv)])
                    sl = sq[:, (c % 2) * 512:(c % 2) * 512 + 512]
                    acv = em.op("scalar", lambda e, c=c, t0=t0, sl=sl: e.activation(
                        out=sl, in_=xsl(c, t0, 512), func=AF.Square),
                        wait=[("DV", zdv[c])] +
                        ([("PE", mml[c % 2])] if c % 2 in mml else []))
                    pemm(pb[2][:], sumsel, xsl(c, t0, 512), c == 0, c == NCH - 1,
                         wait=([("DV", zdv[c])] + (bwait(2) if c == 0 else [])),
                         inc=True)
                    mml[c % 2] = pemm(pb[3][:], sumsel, sl, c == 0, c == NCH - 1,
                                      wait=[("AC", acv)] + (bwait(3) if c == 0
                                                            else []), inc=True)
                pev = em.cnt["PE"]
                murow = rows[0:1, 0:512]
                invrow = rows[0:1, 512:1024]
                tmprow = rows[0:1, 1024:1536]
                d1 = em.op("vector", lambda e: e.tensor_scalar(
                    out=murow, in0=pb[2][0:1, :], scalar1=1.0 / D,
                    scalar2=None, op0=ALU.mult), wait=[("PE", pev)])
                bankdep[2] = ("DV", d1)
                em.op("vector", lambda e: e.tensor_tensor(
                    out=tmprow, in0=murow, in1=murow, op=ALU.mult))
                d3 = em.op("vector", lambda e: e.scalar_tensor_tensor(
                    out=invrow, in0=pb[3][0:1, :], scalar=1.0 / D,
                    in1=tmprow, op0=ALU.mult, op1=ALU.subtract))
                bankdep[3] = ("DV", d3)
                a1 = em.op("scalar", lambda e: e.activation(
                    out=invrow, in_=invrow, func=AF.Sqrt,
                    bias=col_eps5[0:1, 0:1]), wait=[("DV", d3)])
                em.op("vector", lambda e: e.reciprocal(
                    out=invrow, in_=invrow), wait=[("AC", a1)])
                d5 = em.op("vector", lambda e: e.tensor_tensor(
                    out=murow, in0=murow, in1=invrow, op=ALU.mult))
                p1 = pemm(pb[4][:], onesrow, invrow, True, True,
                          wait=[("DV", d5)] + bwait(4), inc=True)
                p2 = pemm(pb[5][:], onesrow, murow, True, True,
                          wait=bwait(5), inc=True)
                for c in range(NCH):
                    em.op("vector", lambda e, c=c, t0=t0: e.tensor_tensor(
                        out=scrm, in0=xsl(c, t0, 512), in1=pb[4][:], op=ALU.mult),
                        wait=[("PE", p2)])
                    dvx = em.op("vector", lambda e, c=c, t0=t0: e.tensor_tensor(
                        out=xsl(c, t0, 512), in0=scrm, in1=pb[5][:],
                        op=ALU.subtract))
                bankdep[4] = ("DV", em.cnt["DV"])
                bankdep[5] = ("DV", em.cnt["DV"])
            A.pop()

        emit_ln(dr["xT"])
        if upto == "postln":
            finish()
            return nc, em

        x3dv = em.cnt["DV"]
        for c in range(NCH):
            em.dma("gpsimd", dr["x3sp"][c * 128:(c + 1) * 128, :], xsl(c, 0, S),
                   "XSP", wait=[("DV", x3dv)] if c == 0 else ())
        x3sp_done = em.cnt["XSP"]

        A.push()
        nrow = A.take(1536)
        phase_w = [("DV", em.cnt["DV"]), ("PE", em.cnt["PE"]), ("AC", em.cnt["AC"]),
                   ("GP", em.cnt["GP"])]
        lv = em.dma("sync", tabs[:, 0:2 * S], dr["c_xpos"][:, :], "LDV",
                    wait=phase_w)
        qrow = nrow[0:1, 0:2 * DEPTH * HD]
        qksp = nrow[0:1, 1024:1536]
        lq = em.dma("sync", qrow, dr["f_qksrow"][0:1, :], "LDV")
        for L in range(DEPTH):
            em.op("vector", lambda e, L=L: e.reduce_max(
                out=qksp[:, L:L + 1], in_=qrow[:, L * HD:(L + 1) * HD],
                axis=mybir.AxisListType.X), wait=[("LDV", lq)])
            em.op("vector", lambda e, L=L: e.reduce_max(
                out=qksp[:, DEPTH + L:DEPTH + L + 1],
                in_=qrow[:, DEPTH * HD + L * HD:DEPTH * HD + (L + 1) * HD],
                axis=mybir.AxisListType.X))
            em.op("vector", lambda e, L=L: e.tensor_tensor(
                out=qksp[:, 12 + L:13 + L], in0=qksp[:, L:L + 1],
                in1=qksp[:, DEPTH + L:DEPTH + L + 1], op=ALU.mult))
            dvc_ = em.op("vector", lambda e, L=L: e.tensor_scalar(
                out=qksp[:, 12 + L:13 + L], in0=qksp[:, 12 + L:13 + L],
                scalar1=-8.0, scalar2=None, op0=ALU.mult))
            pc_ = pemm(pb[7][:, L:L + 1], onesrow, qksp[:, 12 + L:13 + L],
                       True, True, wait=[("DV", dvc_), ("GP", gp_init2)] +
                       (bwait(7) if L == 0 else []), inc=True)
        dvnc = em.op("vector", lambda e: e.tensor_copy(csm[:, 40:40 + DEPTH],
                                                       pb[7][:, 0:DEPTH]),
                     wait=[("PE", em.cnt["PE"])])
        bankdep[7] = ("DV", dvnc)
        xpos_ld = em.cnt["LDV"]
        A.pop()

        if upto == "xpos":
            finish()
            return nc, em

        A.push()
        rn16F = A.take(512)
        dcolA = A.take(16)
        dcolB = A.take(16)
        ov0 = A.off
        kwin = A.take(8 * 1024)
        vring = A.take(6 * 1040)
        quF = A.take(8 * 512)
        aoF = A.take(8 * 256)
        sqF = A.take(1024)
        pringF = sqF
        szA = A.off - ov0
        A.off = ov0
        hSh = takeB(NH2 * 512)
        hSl = takeB(NH2 * 512)
        xnh = takeB(8 * 512)
        xnl = takeB(8 * 512)
        wb3h = takeB(2 * 1024)
        wb3l = takeB(2 * 1024)
        hstg = A.take(512)
        gtF = A.take(1024)
        rowsF = A.take(1536)
        lncolp = A.take(16)
        sqB = A.take(1024)
        A.off = ov0 + max(szA, A.off - ov0)
        em.op("gpsimd", lambda e: e.memset(rn16F, 0.0),
              wait=[("DV", em.cnt["DV"]), ("PE", em.cnt["PE"])])
        gpF0 = em.cnt["GP"]
        BT = [(0, 512), (512, 512), (1024, 512), (1536, 512)]

        def ffn_layer(L):
            for sl_ in range(6):
                em.op("gpsimd", lambda e, sl_=sl_: e.memset(
                    vring[:, sl_ * 1040 + 64:(sl_ + 1) * 1040:65], 1.0),
                    wait=[("DV", em.cnt["DV"]), ("PE", em.cnt["PE"])]
                    if sl_ == 0 else ())
            gpF = em.cnt["GP"]
            qscol = qkc[:, L:L + 1]
            kscol = qkc[:, DEPTH + L:DEPTH + L + 1]
            negc = csm[:, 40 + L:41 + L]

            prev_proj_pe = {"v": 0}
            dcol_pending = {"v": None}
            for st in range(4):
                t0 = st * 512
                xblkdv = em.cnt["DV"]
                prev_proj_pe["v"] = em.cnt["PE"]
                def proj_qk(wdram, scol, At, Bt, SLt, dsts):
                    for mp in range(4):
                        mcs = [2 * mp, 2 * mp + 1]
                        pe_raw = []
                        for i, mc in enumerate(mcs):
                            waps, wwait, wpar = wload(wdram, mc, 1024)
                            for k in range(NCH):
                                pemm(pb[i][:], waps[:, k * 128:(k + 1) * 128],
                                     xsl(k, t0, 512, shift=(k >= 4)),
                                     k == 0, k == NCH - 1,
                                     wait=(wwait + bwait(i) +
                                           [("DV", xblkdv)]) if k == 0 else ())
                            wfree(wpar)
                            pe_raw.append(em.cnt["PE"])
                        for i, mc in enumerate(mcs):
                            acs = em.op("scalar", lambda e, i=i: e.activation(
                                out=sqF[:, 512:1024], in_=pb[i][:],
                                func=AF.Square),
                                wait=[("PE", pe_raw[i]), ("PE", em.cnt["PE"])])
                            pemm(pb[4][:], band_ffn(mc), sqF[:, 512:1024],
                                 mc == 0, mc == NCH - 1,
                                 wait=[("AC", acs)] + (bwait(4) if mc == 0
                                                       else []), inc=True)
                            dvz = em.op("vector", lambda e, i=i, scol=scol:
                                        e.tensor_scalar(
                                out=pb[2 + i][:], in0=pb[i][:],
                                scalar1=scol, scalar2=None, op0=ALU.mult),
                                wait=[("AC", acs)] + bwait(2 + i))
                            bankdep[i] = ("DV", dvz)
                            d = dsts[mc]
                            em.op("vector", lambda e, i=i, d=d, t0=t0, At=At:
                                  e.tensor_tensor(
                                out=d, in0=pb[2 + i][:],
                                in1=tabs[:, At * S + t0:At * S + t0 + 512],
                                op=ALU.mult),
                                wait=[("PE", prev_proj_pe["v"])])
                            for qd in range(4):
                                pq, ps = qd * 32, (qd ^ 1) * 32
                                em.op("vector", lambda e, i=i, pq=pq, ps=ps,
                                      t0=t0, Bt=Bt: e.tensor_tensor(
                                    out=sqF[pq:pq + 32, 0:512],
                                    in0=pb[2 + i][ps:ps + 32, :],
                                    in1=tabs[pq:pq + 32,
                                             Bt * S + t0:Bt * S + t0 + 512],
                                    op=ALU.mult))
                            dvu = em.op("vector", lambda e, d=d: e.tensor_tensor(
                                out=d, in0=d, in1=sqF[:, 0:512], op=ALU.add))
                            bankdep[2 + i] = ("DV", dvu)
                            em.op("vector", lambda e, d=d, t0=t0, SLt=SLt:
                                  e.tensor_tensor(
                                out=d, in0=d,
                                in1=tabs[:, SLt * S + t0:SLt * S + t0 + 512],
                                op=ALU.mult), wait=[("LDV", xpos_ld)])
                    pev = em.cnt["PE"]
                    a1 = em.op("scalar", lambda e: e.activation(
                        out=rn16F[0:16, 0:512], in_=pb[4][0:16, :],
                        func=AF.Sqrt), wait=[("PE", pev), ("GP", gpF),
                                             ("DV", em.cnt["DV"])])
                    bankdep[4] = ("AC", a1)
                    em.op("vector", lambda e: e.tensor_scalar(
                        out=rn16F[0:16, 0:512], in0=rn16F[0:16, 0:512],
                        scalar1=1e-12, scalar2=None, op0=ALU.max),
                        wait=[("AC", a1)])
                    d3 = em.op("vector", lambda e: e.reciprocal(
                        out=rn16F[0:16, 0:512], in_=rn16F[0:16, 0:512]))
                    for mc in range(NCH):
                        pevb = pemm(pb[5][:],
                                    selc[0:16, (8 + mc) * 128:(9 + mc) * 128],
                                    rn16F[0:16, 0:512], True, True,
                                    wait=[("DV", d3)] + bwait(5),
                                    inc=True)
                        dvm = em.op("vector", lambda e, d=dsts[mc]:
                                    e.tensor_tensor(out=d, in0=d,
                                                    in1=pb[5][:],
                                                    op=ALU.mult),
                                    wait=[("PE", pevb)])
                        bankdep[5] = ("DV", dvm)

                qdsts = [quF[:, mc * 512:(mc + 1) * 512] for mc in range(NCH)]
                proj_qk(dr["f_wq"][L], qscol, 2, 3, 0, qdsts)
                slot0 = (2 * st) % 4
                kdsts = [kwin[:, mc * 1024 + slot0 * 256:
                              mc * 1024 + slot0 * 256 + 512]
                         for mc in range(NCH)]
                proj_qk(dr["f_wk"][L], kscol, 2, 3, 1, kdsts)
                for kc in range(NCH):
                    waps, wwait, wpar = wload(dr["f_wv"][L], kc, 1024)
                    for tc in range(4):
                        for nd in range(2):
                            pemm(pb[2 * tc + nd][:],
                                 xsl(kc, t0 + tc * 128, 128, shift=(kc >= 4)),
                                 waps[:, nd * 512:(nd + 1) * 512],
                                 kc == 0, kc == NCH - 1,
                                 wait=(wwait + bwait(2 * tc + nd) +
                                       [("DV", em.cnt["DV"])])
                                 if kc == 0 else wwait, inc=True)
                    wfree(wpar)
                pev = em.cnt["PE"]
                for tc in range(4):
                    tg = (4 * st + tc) % 6
                    for nd in range(2):
                        dvv = em.op("vector", lambda e, tg=tg, nd=nd, tc=tc:
                                    e.tensor_copy(
                            vring[:, tg * 1040:(tg + 1) * 1040].rearrange(
                                "p (kv d) -> p kv d", kv=16, d=65)
                            [:, nd * 8:(nd + 1) * 8, 0:64],
                            pb[2 * tc + nd][:].rearrange(
                                "p (kv d) -> p kv d", kv=8, d=64)),
                            wait=[("PE", pev), ("GP", gpF)])
                        bankdep[2 * tc + nd] = ("DV", dvv)
                if dcol_pending["v"] is not None:
                    pdv, pcol = dcol_pending["v"]
                    for c in range(4, NCH):
                        em.op("vector", lambda e, c=c, pcol=pcol: e.tensor_tensor(
                            out=xsl(c, pcol, 1), in0=xsl(c, pcol, 1),
                            in1=dcolA[:, c:c + 1], op=ALU.add),
                            wait=[("PE", em.cnt["PE"]), ("DV", pdv)])
                    dcol_pending["v"] = None
                blkdv = em.cnt["DV"]
                for b2 in range(2):
                    blk = 2 * st + b2
                    bt0 = blk * 256
                    kminc = max(0, (bt0 - 256) // 128)
                    kmaxc = (bt0 + 256) // 128
                    nkc = kmaxc - kminc
                    for pc in range(NCH):
                        pdep = {}
                        pvpe = {}

                        def emit_s(ki):
                            kc = kminc + ki
                            ri = ki % 2
                            delta = kc * 128 - bt0
                            kblk = kc // 2
                            pcol = None
                            if delta > -128:
                                pcol = pen_ge(delta)
                            if delta < -1:
                                pcol = pen_le(delta + 256)
                            for hi in range(2):
                                base = 64 * hi
                                bank = pb[2 + hi]
                                kap = kwin[base:base + 64,
                                           pc * 1024 + (kblk % 4) * 256 +
                                           (kc % 2) * 128:
                                           pc * 1024 + (kblk % 4) * 256 +
                                           (kc % 2) * 128 + 128]
                                sv = pemm(bank[:, 0:256], kap,
                                          quF[base:base + 64,
                                              pc * 512 + b2 * 256:
                                              pc * 512 + b2 * 256 + 256],
                                          True, True,
                                          wait=bwait(2 + hi) +
                                          ([("DV", blkdv)] if ki == 0 else []),
                                          inc=True)
                                po = (ri * 2 + hi) * 256
                                acx = em.op("scalar", lambda e, bank=bank, po=po:
                                            e.activation(
                                    out=pringF[:, po:po + 256],
                                    in_=bank[:, 0:256],
                                    func=AF.Exp, bias=negc[:, 0:1], scale=8.0),
                                    wait=[("PE", sv)] +
                                    ([("PE", pvpe[(ki - 2, hi)])]
                                     if (ki - 2, hi) in pvpe else []))
                                bankdep[2 + hi] = ("AC", acx)
                                if pcol is not None:
                                    gpm = em.op("gpsimd", lambda e, po=po,
                                                pcol=pcol: e.tensor_tensor(
                                        out=pringF[:, po:po + 256],
                                        in0=pringF[:, po:po + 256],
                                        in1=seli[:, pcol:pcol + 256],
                                        op=ALU.mult),
                                        wait=[("AC", acx)])
                                    pdep[(ki, hi)] = [("GP", gpm)]
                                else:
                                    pdep[(ki, hi)] = [("AC", acx)]

                        def emit_pv(ki):
                            kc = kminc + ki
                            ri = ki % 2
                            for hi in range(2):
                                kv = 2 * pc + hi
                                po = (ri * 2 + hi) * 256
                                vbase = (kc % 6) * 1040 + kv * 65
                                for half in range(2):
                                    bnk = pb[4 + 2 * hi + half]
                                    pvpe[(ki, hi)] = pemm(
                                        bnk[0:65, 0:256],
                                        vring[64 * half:64 * half + 64,
                                              vbase:vbase + 65],
                                        pringF[64 * half:64 * half + 64,
                                               po:po + 256],
                                        ki == 0, ki == nkc - 1,
                                        wait=pdep[(ki, hi)] +
                                        (bwait(4 + 2 * hi + half) if ki == 0
                                         else []), inc=True)

                        for ki in range(nkc):
                            emit_s(ki)
                            if ki >= 1:
                                emit_pv(ki - 1)
                        emit_pv(nkc - 1)
                        for hi in range(2):
                            em.op("vector", lambda e, hi=hi: e.tensor_copy(
                                rn16F[64 + 32 * hi:65 + 32 * hi, 256:512],
                                pb[4 + 2 * hi][64:65, 0:256]),
                                wait=[("PE", pvpe[(nkc - 1, hi)])])
                            em.op("vector", lambda e, hi=hi: e.tensor_tensor(
                                out=rn16F[64 + 32 * hi:65 + 32 * hi, 256:512],
                                in0=rn16F[64 + 32 * hi:65 + 32 * hi, 256:512],
                                in1=pb[5 + 2 * hi][64:65, 0:256], op=ALU.add))
                            em.op("vector", lambda e, hi=hi: e.reciprocal(
                                out=rn16F[64 + 32 * hi:65 + 32 * hi, 256:512],
                                in_=rn16F[64 + 32 * hi:65 + 32 * hi, 256:512]))
                        dvr = em.cnt["DV"]
                        for hi in range(2):
                            em.op("vector", lambda e, hi=hi, pc=pc:
                                  e.tensor_copy(
                                aoF[64 * hi:64 * hi + 64,
                                    pc * 256:pc * 256 + 256],
                                pb[4 + 2 * hi][0:64, 0:256]))
                            em.op("vector", lambda e, hi=hi, pc=pc:
                                  e.tensor_tensor(
                                out=aoF[64 * hi:64 * hi + 64,
                                        pc * 256:pc * 256 + 256],
                                in0=aoF[64 * hi:64 * hi + 64,
                                        pc * 256:pc * 256 + 256],
                                in1=pb[5 + 2 * hi][0:64, 0:256], op=ALU.add))
                        pevb = pemm(pb[2][:, 0:256], selpair,
                                    rn16F[:, 256:512], True, True,
                                    wait=[("DV", dvr)] + bwait(2),
                                    inc=True)
                        dvo = em.op("vector", lambda e, pc=pc: e.tensor_tensor(
                            out=aoF[:, pc * 256:pc * 256 + 256],
                            in0=aoF[:, pc * 256:pc * 256 + 256],
                            in1=pb[2][:, 0:256], op=ALU.mult),
                            wait=[("PE", pevb)])
                        bankdep[2] = ("DV", dvo)
                        for i_ in range(4, 8):
                            bankdep[i_] = ("DV", dvo)
                    aodep = ("DV", em.cnt["DV"])
                    for mc in range(NCH):
                        waps, wwait, wpar = wload(dr["f_wo"][L], mc, 1024)
                        bk = mc % 2
                        for k in range(NCH):
                            pemm(pb[bk][:, 0:256],
                                 waps[:, k * 128:(k + 1) * 128],
                                 aoF[:, k * 256:(k + 1) * 256], k == 0,
                                 k == NCH - 1,
                                 wait=(wwait + bwait(bk) + [aodep])
                                 if k == 0 else ())
                        wfree(wpar)
                        pev = em.cnt["PE"]
                        defer = (mc >= 4 and b2 == 1 and st < 3)
                        wdt = 255 if defer else 256
                        dvx = em.op("vector", lambda e, mc=mc, bt0=bt0, bk=bk,
                                    wdt=wdt: e.tensor_tensor(
                            out=xsl(mc, bt0, wdt), in0=xsl(mc, bt0, wdt),
                            in1=pb[bk][:, 0:wdt], op=ALU.add),
                            wait=[("PE", pev), ("XSP", x3sp_done)])
                        if defer:
                            dvx = em.op("vector", lambda e, mc=mc, bk=bk:
                                        e.tensor_copy(dcolA[:, mc:mc + 1],
                                                      pb[bk][:, 255:256]))
                        bankdep[bk] = ("DV", dvx)
                    if b2 == 1 and st < 3:
                        dcol_pending["v"] = (em.cnt["DV"], bt0 + 255)

            if upto == "ffnattn0" and L == 0:
                return
            lnw = em.dma("sync", lncolp, dr["f_lnrow"][L], "LDX",
                         wait=[("PE", em.cnt["PE"]), ("DV", em.cnt["DV"])])
            b1w = em.dma("sync", b1c[:, :], dr["f_b1"][L], "LDX")
            lnw = b1w
            dcolB_pending = {"v": None}
            nst = len(BT)
            for st, (T0, W) in enumerate(BT):
                mml = {}
                for c in range(NCH):
                    sl = sqB[:, (c % 2) * 512:(c % 2) * 512 + W]
                    acv = em.op("scalar", lambda e, c=c, T0=T0, W=W, sl=sl:
                                e.activation(
                        out=sl, in_=xsl(c, T0, W, shift=(c >= 4)),
                        func=AF.Square),
                        wait=[("DV", em.cnt["DV"]), ("PE", em.cnt["PE"]),
                              ("GP", em.cnt["GP"])]
                        if c == 0 else
                        ([("PE", mml[c % 2])] if c % 2 in mml else []))
                    pemm(pb[2][:, 0:W], sumsel, xsl(c, T0, W, shift=(c >= 4)),
                         c == 0, c == NCH - 1,
                         wait=(bwait(2) + [("DV", em.cnt["DV"])]) if c == 0
                         else [], inc=True)
                    mml[c % 2] = pemm(pb[3][:, 0:W], sumsel, sl, c == 0,
                                      c == NCH - 1,
                                      wait=[("AC", acv)] + (bwait(3) if c == 0
                                                            else []), inc=True)
                pev = em.cnt["PE"]
                murow = rowsF[0:1, 0:512]
                invrow = rowsF[0:1, 512:1024]
                tmprow = rowsF[0:1, 1024:1536]
                d1 = em.op("vector", lambda e, W=W: e.tensor_scalar(
                    out=murow[:, 0:W], in0=pb[2][0:1, 0:W], scalar1=1.0 / D,
                    scalar2=None, op0=ALU.mult), wait=[("PE", pev)])
                bankdep[2] = ("DV", d1)
                em.op("vector", lambda e, W=W: e.tensor_tensor(
                    out=tmprow[:, 0:W], in0=murow[:, 0:W], in1=murow[:, 0:W],
                    op=ALU.mult))
                d3 = em.op("vector", lambda e, W=W: e.scalar_tensor_tensor(
                    out=invrow[:, 0:W], in0=pb[3][0:1, 0:W], scalar=1.0 / D,
                    in1=tmprow[:, 0:W], op0=ALU.mult, op1=ALU.subtract))
                bankdep[3] = ("DV", d3)
                a1 = em.op("scalar", lambda e, W=W: e.activation(
                    out=invrow[:, 0:W], in_=invrow[:, 0:W], func=AF.Sqrt,
                    bias=col_eps5[0:1, 0:1]), wait=[("DV", d3)])
                d4 = em.op("vector", lambda e, W=W: e.reciprocal(
                    out=invrow[:, 0:W], in_=invrow[:, 0:W]), wait=[("AC", a1)])
                p1 = pemm(pb[4][:, 0:W], onesrow, murow[:, 0:W], True, True,
                          wait=[("DV", d4)] + bwait(4), inc=True)
                p2 = pemm(pb[5][:, 0:W], onesrow, invrow[:, 0:W], True, True,
                          wait=bwait(5), inc=True)
                xstg = sqB[:, 0:512]
                for c in range(NCH):
                    em.op("vector", lambda e, c=c, T0=T0, W=W: e.tensor_tensor(
                        out=xstg[:, 0:W],
                        in0=xsl(c, T0, W, shift=(c >= 4)), in1=pb[4][:, 0:W],
                        op=ALU.subtract), wait=[("PE", p2)])
                    em.op("vector", lambda e, W=W: e.tensor_tensor(
                        out=xstg[:, 0:W], in0=xstg[:, 0:W], in1=pb[5][:, 0:W],
                        op=ALU.mult))
                    dxn = em.op("vector", lambda e, c=c, W=W: e.tensor_scalar(
                        out=xstg[:, 0:W], in0=xstg[:, 0:W],
                        scalar1=lncolp[:, c:c + 1],
                        scalar2=lncolp[:, 8 + c:9 + c],
                        op0=ALU.mult, op1=ALU.add),
                        wait=[("LDX", lnw)])
                    hv = em.op("gpsimd", lambda e, c=c, W=W: e.tensor_copy(
                        xnh[:, c * 512:c * 512 + W], xstg[:, 0:W]),
                        wait=[("DV", dxn)])
                    em.op("vector", lambda e, c=c, W=W:
                          e.scalar_tensor_tensor(
                        out=xnl[:, c * 512:c * 512 + W],
                        in0=xnh[:, c * 512:c * 512 + W], scalar=-1.0,
                        in1=xstg[:, 0:W], op0=ALU.mult, op1=ALU.add),
                        wait=[("GP", hv)])
                    bankdep[4] = ("DV", dxn)
                    bankdep[5] = ("DV", dxn)
                xndv = em.cnt["DV"]
                xngp = em.cnt["GP"]
                if dcolB_pending["v"] is not None:
                    pdv, pcol = dcolB_pending["v"]
                    for c in range(4, NCH):
                        em.op("vector", lambda e, c=c, pcol=pcol:
                              e.tensor_tensor(
                            out=xsl(c, pcol, 1), in0=xsl(c, pcol, 1),
                            in1=dcolB[:, c:c + 1], op=ALU.add),
                            wait=[("DV", pdv)])
                    dcolB_pending["v"] = None
                for cc in range(NH2):
                    whA, wlA, wwaitA, wparA = wload3(
                        dr["f_w1"][L][cc, :, 0:1024], slice(None), 1024)
                    whG, wlG, wwaitG, wparG = wload3(
                        dr["f_w1"][L][cc, :, 1024:2048], slice(None), 1024)
                    bi = cc % 2
                    for k in range(NCH):
                        mm3(pb[0 + bi][:, 0:W], whA[:, k * 128:(k + 1) * 128],
                            wlA[:, k * 128:(k + 1) * 128],
                            xnh[:, k * 512:k * 512 + W],
                            xnl[:, k * 512:k * 512 + W], k, NCH,
                            wwaitA + bwait(bi) +
                            [("DV", xndv), ("GP", xngp)])
                    wfree3(wparA)
                    pea = em.cnt["PE"]
                    for k in range(NCH):
                        mm3(pb[2 + bi][:, 0:W], whG[:, k * 128:(k + 1) * 128],
                            wlG[:, k * 128:(k + 1) * 128],
                            xnh[:, k * 512:k * 512 + W],
                            xnl[:, k * 512:k * 512 + W], k, NCH,
                            wwaitG + bwait(2 + bi))
                    wfree3(wparG)
                    peg = em.cnt["PE"]
                    acg = em.op("scalar", lambda e, bi=bi, cc=cc, W=W:
                                e.activation(
                        out=gtF[:, bi * 512:bi * 512 + W],
                        in_=pb[2 + bi][:, 0:W],
                        func=AF.Gelu, bias=b1c[:, NH2 + cc:NH2 + cc + 1]),
                        wait=[("PE", peg), ("LDX", b1w)])
                    bankdep[2 + bi] = ("AC", acg)
                    dvh = em.op("vector", lambda e, bi=bi, cc=cc, W=W:
                                e.scalar_tensor_tensor(
                        out=hstg[:, 0:W],
                        in0=pb[0 + bi][:, 0:W],
                        scalar=b1c[:, cc:cc + 1],
                        in1=gtF[:, bi * 512:bi * 512 + W],
                        op0=ALU.add, op1=ALU.mult),
                        wait=[("PE", pea), ("AC", acg)])
                    bankdep[0 + bi] = ("DV", dvh)
                    hv = em.op("gpsimd", lambda e, cc=cc, W=W: e.tensor_copy(
                        hSh[:, cc * 512:cc * 512 + W], hstg[:, 0:W]),
                        wait=[("DV", dvh)])
                    em.op("vector", lambda e, cc=cc, W=W:
                          e.scalar_tensor_tensor(
                        out=hSl[:, cc * 512:cc * 512 + W],
                        in0=hSh[:, cc * 512:cc * 512 + W], scalar=-1.0,
                        in1=hstg[:, 0:W], op0=ALU.mult, op1=ALU.add),
                        wait=[("GP", hv)])
                hdv = em.cnt["DV"]
                hgp = em.cnt["GP"]
                W2P = ((0, 1024), (1024, 2048), (2048, DIP))
                for m in range(NCH):
                    parts = [wload3(dr["f_w2"][L][m, :, 0:1024], slice(None),
                                    1024),
                             wload3(dr["f_w2"][L][m, :, 1024:2048], slice(None),
                                    1024), None]
                    bk = 4 + (m % 2)
                    for k in range(NH2):
                        pi = min(k // 8, 2)
                        wh_, wl_, wwait_, wpar_ = parts[pi]
                        koff = k - 8 * pi
                        w0 = ((wwait_ + bwait(bk) +
                               [("DV", hdv), ("GP", hgp)])
                              if k == 0 else
                              (wwait_ if k in (8, 16) else ()))
                        pemm(pb[bk][:, 0:W],
                             wh_[:, koff * 128:(koff + 1) * 128],
                             hSh[:, k * 512:k * 512 + W], k == 0, False,
                             wait=w0, inc=True)
                        pemm(pb[bk][:, 0:W],
                             wh_[:, koff * 128:(koff + 1) * 128],
                             hSl[:, k * 512:k * 512 + W], False, False,
                             inc=True)
                        pemm(pb[bk][:, 0:W],
                             wl_[:, koff * 128:(koff + 1) * 128],
                             hSh[:, k * 512:k * 512 + W], False, k == NH2 - 1,
                             inc=True)
                        if k == 7:
                            wfree3(parts[0][3])
                            parts[2] = wload3(dr["f_w2"][L][m, :, 2048:DIP],
                                              slice(None), DIP - 2048)
                        if k == 15:
                            wfree3(parts[1][3])
                    wfree3(parts[2][3])
                    pev = em.cnt["PE"]
                    defer = (m >= 4 and st < nst - 1)
                    wdt = W - 1 if defer else W
                    dvx = em.op("vector", lambda e, m=m, T0=T0, bk=bk, wdt=wdt:
                                e.scalar_tensor_tensor(
                        out=xsl(m, T0, wdt), in0=pb[bk][:, 0:wdt],
                        scalar=b2c[:, L * 8 + m:L * 8 + m + 1],
                        in1=xsl(m, T0, wdt), op0=ALU.add, op1=ALU.add),
                        wait=[("PE", pev)])
                    if defer:
                        dvx = em.op("vector", lambda e, m=m, bk=bk, L=L, W=W:
                                    e.tensor_scalar(
                            out=dcolB[:, m:m + 1], in0=pb[bk][:, W - 1:W],
                            scalar1=b2c[:, L * 8 + m:L * 8 + m + 1],
                            scalar2=None, op0=ALU.add))
                    bankdep[bk] = ("DV", dvx)
                if st < nst - 1:
                    dcolB_pending["v"] = (em.cnt["DV"], T0 + W - 1)

        for L in range(DEPTH):
            ffn_layer(L)
            if upto == ("ffn%d" % L) or upto == "ffnattn0":
                break

        if upto is not None and upto.startswith("ffn"):
            finish()
            return nc, em
        A.pop()

        emit_ln(dr["x3sp"])

        finish()
        return nc, em

def kernel(**inputs):
    consts = host_consts()
    wts = host_weights(inputs)
    hid = np.asarray(inputs["L0_attn_hidden"])
    nc, _ = build_program()
    in_maps = []
    for b in range(B):
        m = {"xT": np.ascontiguousarray(hid[b].T)}
        m.update(consts)
        m.update(wts)
        in_maps.append(m)
    res = run_bass_kernel_spmd(nc, in_maps, core_ids=list(range(B)))
    out = np.stack([np.ascontiguousarray(r["out"].T) for r in res.results])
    return out


# version 12
# speedup vs baseline: 1.2081x; 1.2081x over previous
import math
import numpy as np

import concourse.bass as bass
import concourse.mybir as mybir
from concourse.bass_utils import run_bass_kernel_spmd

F32 = mybir.dt.float32
I32 = mybir.dt.int32
AF = mybir.ActivationFunctionType
ALU = mybir.AluOpType

B = 8
S = 2048
D = 1024
NCH = D // 128
HD = 64
H = 16
KVH = 4
LAT = 256
DI = 2730
DIP = 2816
NH2 = DIP // 128
DEPTH = 6
XP = 2052

PAIRS = [(0, 4), (1, 5), (2, 6), (3, 7), (8, 12), (9, 13), (10, 14), (11, 15)]

def _split(v):
    hi = np.float32(v)
    lo = np.float32(v - np.float64(hi))
    return float(hi), float(lo)

PI_HI, PI_LO = _split(math.pi)
IPI_HI, IPI_LO = _split(1.0 / math.pi)


ENG_SEM = {"vector": "DV", "scalar": "AC", "gpsimd": "GP"}


class Emit:
    ENGS = ("sync", "tensor", "vector", "scalar", "gpsimd")

    def __init__(self):
        self.ops = {e: [] for e in self.ENGS}
        self.cnt = {}
        self.sems = {}
        self.mark = {}

    def sem(self, name):
        self.cnt[name] = 0

    def op(self, eng, fn, wait=(), inc=None, chain=True):
        wait = [(s, v) for (s, v) in wait if v > 0]
        es = ENG_SEM.get(eng)
        if es is not None:
            if chain and self.cnt[es] > 0:
                wait = list(wait) + [(es, self.cnt[es])]
            assert inc is None
            inc = (es, 1)
        w = []
        for sname, v in wait:
            key = (eng, sname)
            if v > self.mark.get(key, 0):
                self.mark[key] = v
                w.append((sname, v))
        ret = None
        if inc is not None:
            self.cnt[inc[0]] += inc[1]
            ret = self.cnt[inc[0]]
        self.ops[eng].append((fn, tuple(w), inc))
        return ret

    def dma(self, eng, out_ap, in_ap, sem, wait=()):
        wait = [(s, v) for (s, v) in wait if v > 0]
        w = []
        for sname, v in wait:
            key = (eng, sname)
            if v > self.mark.get(key, 0):
                self.mark[key] = v
                w.append((sname, v))
        self.cnt[sem] += 16
        self.ops[eng].append((
            lambda e, o=out_ap, i=in_ap: e.dma_start(out=o, in_=i),
            tuple(w), (sem, 16)))
        return self.cnt[sem]

    def build(self, block):
        sems = self.sems

        def mk(eng):
            items = self.ops[eng]

            def body(e):
                for fn, waits, inc in items:
                    for sname, v in waits[:-1]:
                        e.wait_ge(sems[sname], v)
                    inst = fn(e)
                    if waits:
                        inst._wait_ge(sems[waits[-1][0]], waits[-1][1])
                    if inc is not None:
                        inst.then_inc(sems[inc[0]], inc[1])
            return body
        block.sync(mk("sync"))
        block.tensor(mk("tensor"))
        block.vector(mk("vector"))
        block.scalar(mk("scalar"))
        block.gpsimd(mk("gpsimd"))


def host_consts():
    c = {}
    t32 = np.arange(S, dtype=np.float32)
    i32 = np.arange(0, HD, 2, dtype=np.float32)
    didx = np.tile(np.arange(32), 4)
    sgn = np.ones((128, 1), np.float32)
    for qq in range(4):
        if (qq % 2) == 0:
            sgn[qq * 32:(qq + 1) * 32] = -1.0
    tabs = np.zeros((128, 4 * S), np.float32)
    for sl, base in ((0, 1.0e6), (2, 1.0e4)):
        invf = (np.float32(1.0) / (np.float32(base) ** (i32 / np.float32(HD)))
                ).astype(np.float32)
        fr = (t32[None, :] * invf[didx][:, None]).astype(np.float32)
        tabs[:, sl * S:(sl + 1) * S] = np.cos(fr)
        tabs[:, (sl + 1) * S:(sl + 2) * S] = np.sin(fr) * sgn
    c["c_tabs"] = tabs
    xp = np.zeros((128, 2 * S), np.float32)
    bs = ((i32 + np.float32(0.4 * HD)) / np.float32(1.4 * HD)).astype(np.float32)
    pw = ((t32 - np.float32(S // 2)) / np.float32(128)).astype(np.float32)
    scl = (bs[didx][:, None] ** pw[None, :]).astype(np.float32)
    xp[:, 0:S] = scl
    xp[:, S:2 * S] = (np.float32(1.0) / scl).astype(np.float32)
    c["c_xpos"] = xp
    m = np.zeros((128, 128), np.float32)
    m[:, 0] = 1.0
    c["c_sumsel"] = m
    band = np.zeros((128, 144), np.float32)
    band[:64, 14] = 1.0
    band[64:, 15] = 1.0
    c["c_band"] = band
    band2 = np.zeros((128, 144), np.float32)
    band2[:64, 11] = 1.0
    band2[64:, 15] = 1.0
    c["c_band_mla"] = band2
    def sel(f):
        mm = np.zeros((128, 128), np.float32)
        for j in range(128):
            mm[f(j), j] = 1.0
        return mm
    c["c_sel_pair"] = sel(lambda j: 64 if j < 64 else 96)
    sel_mla = np.zeros((8, 128, 128), np.float32)
    for cc in range(8):
        hA, hB = PAIRS[cc]
        sel_mla[cc] = sel(lambda j, hA=hA, hB=hB: hA if j < 64 else hB)

    sel_ffn = np.zeros((8, 128, 128), np.float32)
    for cc in range(8):
        sel_ffn[cc] = sel(lambda j, cc=cc: 2 * cc + (j >= 64))

    sel_kv = np.zeros((2, 128, 128), np.float32)
    for mm_ in range(2):
        sel_kv[mm_] = sel(lambda j, m=mm_: 2 * m + (j >= 64))
    selc = np.zeros((16, 16 * 128), np.float32)
    for cc in range(8):
        selc[:, cc * 128:(cc + 1) * 128] = sel_mla[cc][0:16]
        selc[:, (8 + cc) * 128:(9 + cc) * 128] = sel_ffn[cc][0:16]
    c["c_selc"] = selc
    c["c_ident"] = np.eye(128, dtype=np.float32)
    kr = np.arange(128)[:, None]
    jE = np.arange(896)[None, :]
    c["c_maskE"] = (jE >= kr + 384).astype(np.float32)
    jE2 = np.arange(400)[None, :]
    c["c_maskE2"] = (jE2 <= kr + 128).astype(np.float32)
    return c


def blk_lhsT(W, perm_cols=None):
    K, M = W.shape
    if perm_cols is not None:
        W = W[:, perm_cols]
    nk, nm = K // 128, M // 128
    return np.ascontiguousarray(
        W.reshape(nk, 128, nm, 128).transpose(2, 1, 0, 3).reshape(nm, 128, nk * 128))


def perm_mla_cols():
    perm64 = list(range(0, 64, 2)) + list(range(1, 64, 2))
    cols = []
    for hA, hB in PAIRS:
        for h in (hA, hB):
            cols.extend(64 * h + p for p in perm64)
    return np.array(cols)


def perm_kv_cols():
    perm64 = list(range(0, 64, 2)) + list(range(1, 64, 2))
    cols = []
    for kv in range(4):
        cols.extend(64 * kv + p for p in perm64)
    return np.array(cols)


def mla_row_perm():
    rows = []
    for hA, hB in PAIRS:
        for h in (hA, hB):
            rows.extend(range(64 * h, 64 * h + 64))
    return np.array(rows)


def host_weights(inp):
    w = {}
    pq = perm_mla_cols()
    pk = perm_kv_cols()
    pr = mla_row_perm()
    w["g_wq"] = blk_lhsT(np.asarray(inp["L0_attn_g_wq"]), pq)
    w["g_kvd"] = blk_lhsT(np.asarray(inp["L0_attn_g_kv_down"]))
    w["g_kup"] = blk_lhsT(np.asarray(inp["L0_attn_g_k_up"]), pk)
    w["g_vup"] = np.ascontiguousarray(
        np.asarray(inp["L0_attn_g_v_up"]).reshape(2, 128, 256))
    w["g_wo"] = blk_lhsT(np.asarray(inp["L0_attn_g_wo"])[pr])
    w["l_wq"] = blk_lhsT(np.asarray(inp["L0_attn_l_wq"]), pq)
    w["l_wk"] = blk_lhsT(np.asarray(inp["L0_attn_l_wk"]), pk)
    w["l_wv"] = np.ascontiguousarray(
        np.asarray(inp["L0_attn_l_wv"]).reshape(NCH, 128, 256))
    w["l_wo"] = blk_lhsT(np.asarray(inp["L0_attn_l_wo"])[pr])
    n1 = np.asarray(inp["L0_attn_norm1_w"])
    n2 = np.asarray(inp["L0_attn_norm2_w"])
    w["normcols"] = np.ascontiguousarray(
        np.concatenate([n1.reshape(NCH, 128).T, n2.reshape(NCH, 128).T], axis=1))
    Wqkv = np.asarray(inp["L0_ffn_Wqkv"])
    w["f_wq"] = np.stack([blk_lhsT(Wqkv[i, :, 0:D]) for i in range(DEPTH)])
    w["f_wk"] = np.stack([blk_lhsT(Wqkv[i, :, D:2 * D]) for i in range(DEPTH)])
    w["f_wv"] = np.stack([np.ascontiguousarray(
        Wqkv[i, :, 2 * D:3 * D].reshape(NCH, 128, D)) for i in range(DEPTH)])
    Wo = np.asarray(inp["L0_ffn_Wo"])
    w["f_wo"] = np.stack([blk_lhsT(Wo[i]) for i in range(DEPTH)])
    W1 = np.asarray(inp["L0_ffn_W1"])
    W2 = np.asarray(inp["L0_ffn_W2"])
    b1 = np.asarray(inp["L0_ffn_b1"])
    w1blk = np.zeros((DEPTH, NH2, 128, 2048), np.float32)
    w2blk = np.zeros((DEPTH, 8, 128, DIP), np.float32)
    b1blk = np.zeros((DEPTH, 128, 2 * NH2), np.float32)
    for i in range(DEPTH):
        W1a = np.zeros((D, DIP), np.float32); W1a[:, :DI] = W1[i, :, :DI]
        W1g = np.zeros((D, DIP), np.float32); W1g[:, :DI] = W1[i, :, DI:]
        a_ = blk_lhsT(W1a)
        g_ = blk_lhsT(W1g)
        w1blk[i, :, :, 0:1024] = a_
        w1blk[i, :, :, 1024:2048] = g_
        W2p = np.zeros((DIP, D), np.float32); W2p[:DI] = W2[i]
        w2blk[i] = blk_lhsT(W2p)
        b1a = np.zeros((DIP,), np.float32); b1a[:DI] = b1[i, :DI]
        b1g = np.zeros((DIP,), np.float32); b1g[:DI] = b1[i, DI:]
        b1blk[i, :, 0:NH2] = b1a.reshape(NH2, 128).T
        b1blk[i, :, NH2:] = b1g.reshape(NH2, 128).T
    w["f_w1"] = w1blk
    w["f_w2"] = w2blk
    w["f_b1"] = b1blk
    w["f_b2"] = np.ascontiguousarray(
        np.asarray(inp["L0_ffn_b2"]).reshape(DEPTH, 8, 128).transpose(2, 0, 1).reshape(128, DEPTH * 8))
    ln_g = np.asarray(inp["L0_ffn_ln_g"])
    ln_b = np.asarray(inp["L0_ffn_ln_b"])
    lncol = np.zeros((DEPTH, 128, 16), np.float32)
    for L in range(DEPTH):
        lncol[L, :, 0:8] = ln_g[L].reshape(NCH, 128).T
        lncol[L, :, 8:16] = ln_b[L].reshape(NCH, 128).T
    w["f_lnrow"] = lncol
    qs = np.asarray(inp["L0_ffn_q_scale"])
    ks = np.asarray(inp["L0_ffn_k_scale"])
    w["f_qscol"] = np.ascontiguousarray(np.tile(qs.T, (2, 1)))
    w["f_kscol"] = np.ascontiguousarray(np.tile(ks.T, (2, 1)))
    qsr = np.zeros((128, 2 * DEPTH * HD), np.float32)
    qsr[0, :DEPTH * HD] = qs.reshape(-1)
    qsr[0, DEPTH * HD:] = ks.reshape(-1)
    w["f_qksrow"] = qsr
    return w


def build_program(upto=None):
    nc = bass.Bass("TRN2", target_bir_lowering=False, debug=False)
    em = Emit()
    for s in ("LD", "LDV", "LDL", "LDX", "SL",
              "WS", "WS2", "PE", "DV", "AC", "GP", "SP", "ST", "XSP"):
        em.sem(s)

    dr = {}
    dr["xT"] = nc.dram_tensor("xT", [D, S], F32, kind="ExternalInput").ap()
    dr["out"] = nc.dram_tensor("out", [D, S], F32, kind="ExternalOutput").ap()
    dr["x3sp"] = nc.dram_tensor("x3sp", [D, S], F32, kind="Internal").ap()
    for name, shape in [
        ("c_tabs", [128, 4 * S]), ("c_xpos", [128, 2 * S]),
        ("c_sumsel", [128, 128]), ("c_band", [128, 144]), ("c_band_mla", [128, 144]),
        ("c_sel_pair", [128, 128]), ("c_selc", [16, 16 * 128]),
        ("c_ident", [128, 128]),
        ("c_maskE", [128, 896]), ("c_maskE2", [128, 400]),
        ("g_wq", [8, 128, 1024]), ("g_kvd", [2, 128, 1024]),
        ("g_kup", [2, 128, 256]), ("g_vup", [2, 128, 256]),
        ("g_wo", [8, 128, 1024]),
        ("l_wq", [8, 128, 1024]), ("l_wk", [2, 128, 1024]),
        ("l_wv", [NCH, 128, 256]), ("l_wo", [8, 128, 1024]),
        ("normcols", [128, 2 * NCH]),
        ("f_wq", [DEPTH, 8, 128, 1024]), ("f_wk", [DEPTH, 8, 128, 1024]),
        ("f_wv", [DEPTH, NCH, 128, D]), ("f_wo", [DEPTH, 8, 128, 1024]),
        ("f_w1", [DEPTH, NH2, 128, 2048]), ("f_w2", [DEPTH, 8, 128, DIP]),
        ("f_b1", [DEPTH, 128, 2 * NH2]), ("f_b2", [128, DEPTH * 8]),
        ("f_lnrow", [DEPTH, 128, 16]), ("f_qscol", [128, DEPTH]),
        ("f_kscol", [128, DEPTH]), ("f_qksrow", [128, 2 * DEPTH * HD]),
    ]:
        dr[name] = nc.dram_tensor(name, shape, F32, kind="ExternalInput").ap()

    import contextlib
    stack = contextlib.ExitStack()
    sb = {}

    def alloc(name, shape, dtype=F32):
        sb[name] = stack.enter_context(nc.sbuf_tensor(name, shape, dtype))
        return sb[name]

    with stack:
        x = alloc("x", [128, NCH * XP])
        tabs = alloc("tabs", [128, 4 * S])
        wb = alloc("wb", [128, 2 * 1024])
        csm = alloc("csm", [128, 64])
        b1c = alloc("b1c", [128, 2 * NH2])
        b2c = alloc("b2c", [128, DEPTH * 8])
        qkc = alloc("qkc", [128, 2 * DEPTH])
        seli = alloc("seli", [128, 2112])
        selc = alloc("selc", [16, 16 * 128])
        ar = alloc("ar", [128, 22208])
        pb = [stack.enter_context(nc.psum_tensor(f"pb{i}", [128, 512], F32))
              for i in range(8)]

        class Arena:
            def __init__(self, base_ap, size):
                self.ap = base_ap
                self.size = size
                self.off = 0
                self.marks = []

            def take(self, n):
                n = (n + 15) // 16 * 16
                assert self.off + n <= self.size, (self.off, n, self.size)
                v = self.ap[:, self.off:self.off + n]
                self.off += n
                return v

            def push(self):
                self.marks.append(self.off)

            def pop(self):
                self.off = self.marks.pop()
        A = Arena(ar, 22208)

        col_eps6 = csm[:, 0:1]
        col_eps5 = csm[:, 1:2]
        col_neg8 = csm[:, 2:3]
        col_negc = csm[:, 3:4]
        n1col = csm[:, 8:16]
        n2col = csm[:, 16:24]

        def dma(eng, out_ap, in_ap, wait=(), sem="LD"):
            return em.dma(eng, out_ap, in_ap, sem, wait=wait)

        for c in range(NCH):
            dma("sync", x[:, c * XP + 1:c * XP + 1 + S], dr["xT"][c * 128:(c + 1) * 128, :])
        dma("sync", csm[:, 8:24], dr["normcols"][:, :])
        dma("sync", b2c[:, :], dr["f_b2"][:, :])
        dma("sync", qkc[:, 0:DEPTH], dr["f_qscol"][:, :])
        dma("sync", qkc[:, DEPTH:2 * DEPTH], dr["f_kscol"][:, :])
        dma("sync", seli[:, 0:128], dr["c_sumsel"][:, :])
        dma("sync", seli[:, 128:256], dr["c_sel_pair"][:, :])
        dma("sync", seli[:, 384:528], dr["c_band"][:, :])
        dma("sync", seli[:, 528:672], dr["c_band_mla"][:, :])
        dma("sync", seli[:, 672:1568], dr["c_maskE"][:, :])
        dma("sync", seli[:, 1568:1968], dr["c_maskE2"][:, :])
        dma("sync", seli[:, 1968:2096], dr["c_ident"][:, :])
        dma("sync", selc[:, :], dr["c_selc"][:, :])
        sumsel = seli[:, 0:128]
        identm = seli[:, 1968:2096]
        selpair = seli[:, 128:256]

        def band_ffn(c):
            return seli[:, 384 + 14 - 2 * c:384 + 142 - 2 * c]

        def band_mla(c):
            o = (11 - c) if c < 4 else (11 - c - 4)
            return seli[:, 528 + o:528 + o + 128]

        def pen_ge(delta):
            return 672 + 384 - delta

        def pen_le(delta):
            return 1568 + 128 - delta

        for c in range(NCH):
            em.op("gpsimd", lambda e, c=c: e.memset(x[:, c * XP:c * XP + 1], 0.0))
        em.op("gpsimd", lambda e: e.memset(col_eps6, 1e-6))
        em.op("gpsimd", lambda e: e.memset(col_eps5, 1e-5))
        em.op("gpsimd", lambda e: e.memset(col_neg8, -8.0))
        gp_init = em.cnt["GP"]

        dma("sync", tabs[:, :], dr["c_tabs"][:, :])
        ld_tab = ld_init = em.cnt["LD"]
        tabs_dv = 0
        tabs_pe = 0
        tabs_ac = 0

        def xsl(c, t0, w, shift=False):
            o = c * XP + 1 + t0 - (1 if shift else 0)
            return x[:, o:o + w]

        def finish():
            dv = em.cnt["DV"]
            for c in range(NCH):
                em.dma("gpsimd", dr["out"][c * 128:(c + 1) * 128, :],
                       xsl(c, 0, S), "ST", wait=[("DV", dv), ("PE", em.cnt["PE"]),
                                                 ("AC", em.cnt["AC"])])
            em.op("gpsimd", lambda e: e.nop(), wait=[("ST", em.cnt["ST"])])
            for name_ in em.cnt:
                em.sems[name_] = stack.enter_context(nc.semaphore(name_))
            blk = stack.enter_context(nc.Block())
            em.build(blk)

        bankdep = {}

        def bwait(i):
            d = bankdep.get(i)
            return [d] if d else []

        def pemm(out_ap, lhsT, rhs, start, stop, wait=(), inc=False):
            em.op("tensor", lambda e, o=out_ap, l=lhsT, r=rhs, s1=start, s2=stop:
                  e.matmul(o, l, r, start=s1, stop=s2), wait=wait,
                  inc=("PE", 1) if (inc or stop) else None)
            return em.cnt["PE"]

        onesc = csm[0:1, 5:6]
        em.op("gpsimd", lambda e: e.memset(csm[:, 5:6], 1.0))
        onesrow = seli[0:1, 256:384]
        em.op("gpsimd", lambda e: e.memset(seli[0:1, 256:384], 1.0))
        gp_init2 = em.cnt["GP"]

        wsc = {"n": 0, "dep": {0: None, 1: None}}

        def wload(dram_blob, idx, ncols, scale_col=None, extra_wait=()):
            par = wsc["n"] % 2
            wsc["n"] += 1
            dst = wb[:, par * 1024:par * 1024 + ncols]
            sem = "WS" if par == 0 else "WS2"
            w = list(extra_wait)
            if wsc["dep"][par] is not None:
                w.append(wsc["dep"][par])
            wv = em.dma("sync" if par == 0 else "scalar", dst, dram_blob[idx],
                        sem, wait=w)
            wsc["par"] = par
            if scale_col is None:
                return dst, [(sem, wv)], par
            dv = None
            for k in range(ncols // 128):
                dv = em.op("vector", lambda e, k=k, dst=dst: e.tensor_scalar(
                    out=dst[:, k * 128:(k + 1) * 128],
                    in0=dst[:, k * 128:(k + 1) * 128],
                    scalar1=scale_col[:, k:k + 1], scalar2=None, op0=ALU.mult),
                    wait=[(sem, wv)] if k == 0 else ())
            return dst, [("DV", dv)], par

        def wfree(par):
            wsc["dep"][par] = ("PE", em.cnt["PE"])

        def emit_rms(pg, xdep):
            rows, sq = pg["rows"], pg["sq"]
            mmlast = {}
            for tt in range(4):
                t0 = tt * 512
                for c in range(NCH):
                    sl = sq[:, (c % 2) * 512:(c % 2) * 512 + 512]
                    acv = em.op("scalar", lambda e, c=c, t0=t0, sl=sl: e.activation(
                        out=sl, in_=xsl(c, t0, 512), func=AF.Square),
                        wait=list(xdep) + ([("PE", mmlast[c % 2])] if c % 2 in mmlast
                                           else []))
                    mmlast[c % 2] = pemm(pb[2][:], sumsel, sl, c == 0, c == NCH - 1,
                                         wait=[("AC", acv)] + (bwait(2) if c == 0
                                                               else []), inc=True)
                pev = mmlast[(NCH - 1) % 2]
                a2 = em.op("scalar", lambda e, t0=t0: e.activation(
                    out=rows[0:1, t0:t0 + 512], in_=pb[2][0:1, :], func=AF.Sqrt,
                    scale=1.0 / D, bias=col_eps6[0:1, 0:1]), wait=[("PE", pev)])
                bankdep[2] = ("AC", a2)
                em.op("vector", lambda e, t0=t0: e.reciprocal(
                    out=rows[0:1, t0:t0 + 512], in_=rows[0:1, t0:t0 + 512]),
                    wait=[("AC", a2)])
            dvr = em.cnt["DV"]
            for tc in range(16):
                pemm(pb[3][:, tc:tc + 1], rows[0:1, tc * 128:(tc + 1) * 128],
                     onesc, True, True,
                     wait=([("DV", dvr), ("GP", gp_init2)] + bwait(3)) if tc == 0
                     else [], inc=(tc == 15))
            pg["rscoldep"] = em.op("vector", lambda e: e.tensor_copy(
                pg["rscol"], pb[3][:, 0:16]), wait=[("PE", em.cnt["PE"])])
            bankdep[3] = ("DV", pg["rscoldep"])

        def emit_rs16(pg, qt0, w):
            spv = None
            for r in range(16):
                spv = em.dma("gpsimd", pg["rs16"][r:r + 1, 0:w],
                             pg["rows"][0:1, qt0:qt0 + w], "SP",
                             wait=[("DV", em.cnt["DV"])] if r == 0 else ())
            pg["rs16dep"] = em.op("vector", lambda e, w=w: e.tensor_tensor(
                out=pg["rs16sq"][0:16, 0:w], in0=pg["rs16"][0:16, 0:w],
                in1=pg["rs16"][0:16, 0:w], op=ALU.mult), wait=[("SP", spv)])

        def emit_gqa(cfg):
            name = cfg["name"]
            Ct, St, win = cfg["Ct"], cfg["St"], cfg["window"]
            QT = cfg["QT"]
            nqt = S // QT
            A.push()
            pg = {}
            pg["rows"] = A.take(S)
            pg["rs16"] = A.take(512)
            pg["rs16sq"] = A.take(512)
            pg["rscol"] = A.take(16)
            pg["sq"] = A.take(1024)
            pg["rs16dep"] = 0
            kpr = A.take(2 * S)
            vsb = A.take(16 * 260)
            qu = A.take(NCH * QT)
            ao = A.take(NCH * QT)
            rn16 = A.take(512)
            lat = qu[:, 0:1024] if cfg["lat"] else None
            pring = A.take(4 * QT)
            scr = pg["sq"]

            emit_rms(pg, cfg["xdep"])
            gpz = em.op("gpsimd", lambda e: e.memset(rn16, 0.0),
                        wait=[("DV", em.cnt["DV"]), ("PE", em.cnt["PE"]),
                              ("AC", em.cnt["AC"])])
            for tc in range(16):
                em.op("gpsimd", lambda e, tc=tc: e.memset(
                    vsb[:, tc * 260 + 64:(tc + 1) * 260:65], 1.0))
            gp_ones = em.cnt["GP"]

            def rope_qknorm(mcs, t0, w, bandfn, selbase, nrows, banks, pe_done,
                            dsts, extra_dst_wait):
                for i, mc in enumerate(mcs):
                    p = banks[i]
                    em.op("vector", lambda e, p=p, d=dsts[i], t0=t0, w=w:
                          e.tensor_tensor(out=d, in0=p[:, 0:w],
                                          in1=tabs[:, Ct * S + t0:Ct * S + t0 + w],
                                          op=ALU.mult),
                          wait=[("PE", pe_done[i])] + (extra_dst_wait[i] or []))
                    for qd in range(4):
                        pq, ps = qd * 32, (qd ^ 1) * 32
                        em.op("vector", lambda e, p=p, pq=pq, ps=ps, t0=t0, w=w:
                              e.tensor_tensor(
                                  out=scr[pq:pq + 32, 0:w], in0=p[ps:ps + 32, 0:w],
                                  in1=tabs[pq:pq + 32, St * S + t0:St * S + t0 + w],
                                  op=ALU.mult))
                    dvu = em.op("vector", lambda e, d=dsts[i], w=w: e.tensor_tensor(
                        out=d, in0=d, in1=scr[:, 0:w], op=ALU.add))
                    acs = em.op("scalar", lambda e, d=dsts[i], w=w: e.activation(
                        out=scr[:, 512:512 + w], in_=d, func=AF.Square),
                        wait=[("DV", dvu)] + ([("PE", em.cnt["PE"])] if i > 0 else
                                              bwait(4)))
                    pemm(pb[4][:, 0:w], bandfn(mc), scr[:, 512:512 + w],
                         i == 0, i == len(mcs) - 1, wait=[("AC", acs)], inc=True)
                pev = em.cnt["PE"]
                d1 = em.op("vector", lambda e, w=w, n=nrows: e.tensor_tensor(
                    out=rn16[0:n, 0:w], in0=pb[4][0:n, 0:w],
                    in1=pg["rs16sq"][0:n, 0:w], op=ALU.mult),
                    wait=[("PE", pev), ("GP", gpz), ("DV", pg["rs16dep"])])
                bankdep[4] = ("DV", d1)
                a1 = em.op("scalar", lambda e, w=w, n=nrows: e.activation(
                    out=rn16[0:n, 0:w], in_=rn16[0:n, 0:w], func=AF.Sqrt,
                    scale=1.0 / HD, bias=col_eps6[0:n, 0:1]), wait=[("DV", d1)])
                em.op("vector", lambda e, w=w, n=nrows: e.reciprocal(
                    out=rn16[0:n, 0:w], in_=rn16[0:n, 0:w]), wait=[("AC", a1)])
                d3 = em.op("vector", lambda e, w=w, n=nrows: e.tensor_tensor(
                    out=rn16[0:n, 0:w], in0=rn16[0:n, 0:w],
                    in1=pg["rs16"][0:n, 0:w], op=ALU.mult))
                out_dv = []
                for i, mc in enumerate(mcs):
                    pevb = pemm(pb[5][:, 0:w],
                                selc[0:nrows, (selbase + mc) * 128:
                                     (selbase + mc) * 128 + 128],
                                rn16[0:nrows, 0:w], True, True,
                                wait=[("DV", d3)] + bwait(5))
                    dvm = em.op("vector", lambda e, d=dsts[i], w=w: e.tensor_tensor(
                        out=d, in0=d, in1=pb[5][:, 0:w], op=ALU.mult),
                        wait=[("PE", pevb)])
                    bankdep[5] = ("DV", dvm)
                    out_dv.append(dvm)
                return out_dv

            for tt in range(4):
                t0 = tt * 512
                emit_rs16(pg, t0, 512)
                pe_done = []
                if cfg["lat"]:
                    latdv = []
                    for mc in range(2):
                        waps, wwait, wpar = wload(dr["g_kvd"], mc, 1024,
                                                  scale_col=cfg["ncol"])
                        for k in range(NCH):
                            pemm(pb[mc][:], waps[:, k * 128:(k + 1) * 128],
                                 xsl(k, t0, 512), k == 0, k == NCH - 1,
                                 wait=(wwait + bwait(mc)) if k == 0 else ())
                        wfree(wpar)
                        pev = em.cnt["PE"]
                        latdv.append(em.op("vector", lambda e, mc=mc: e.tensor_copy(
                            lat[:, mc * 512:(mc + 1) * 512], pb[mc][:]),
                            wait=[("PE", pev)]))
                        bankdep[mc] = ("DV", latdv[-1])
                    for mc in range(2):
                        waps, wwait, wpar = wload(dr["g_kup"], mc, 256)
                        for kc in range(2):
                            pemm(pb[mc][:], waps[:, kc * 128:(kc + 1) * 128],
                                 lat[:, kc * 512:(kc + 1) * 512], kc == 0, kc == 1,
                                 wait=(wwait + bwait(mc) + [("DV", latdv[1])])
                                 if kc == 0 else ())
                        wfree(wpar)
                        pe_done.append(em.cnt["PE"])
                else:
                    for mc in range(2):
                        waps, wwait, wpar = wload(cfg["wk"], mc, 1024,
                                                  scale_col=cfg["ncol"])
                        for k in range(NCH):
                            pemm(pb[mc][:], waps[:, k * 128:(k + 1) * 128],
                                 xsl(k, t0, 512), k == 0, k == NCH - 1,
                                 wait=(wwait + bwait(mc)) if k == 0 else ())
                        wfree(wpar)
                        pe_done.append(em.cnt["PE"])
                dsts = [kpr[:, mc * S + t0:mc * S + t0 + 512] for mc in range(2)]
                kdvs = rope_qknorm([0, 1], t0, 512, band_ffn, 8, 4,
                                   [pb[0], pb[1]], pe_done, dsts,
                                   [cfg["kdst_wait"](tt, mc) for mc in range(2)])
                bankdep[0] = ("DV", kdvs[0])
                bankdep[1] = ("DV", kdvs[1])
                cfg["kdv"][tt] = kdvs
                for tc in range(4):
                    tca = tt * 4 + tc
                    if cfg["lat"]:
                        for kc in range(2):
                            pemm(pb[6][:, 0:256],
                                 lat[:, kc * 512 + tc * 128:kc * 512 + tc * 128 + 128],
                                 vup_ap[:, kc * 256:(kc + 1) * 256], kc == 0, kc == 1,
                                 wait=(bwait(6) + vup_wait + [("DV", latdv[1])])
                                 if kc == 0 else ())
                    else:
                        for kc in range(NCH):
                            pemm(pb[6][:, 0:256],
                                 xsl(kc, t0 + tc * 128, 128),
                                 lwv_ap[:, kc * 256:(kc + 1) * 256], kc == 0,
                                 kc == NCH - 1,
                                 wait=(bwait(6) + lwv_wait) if kc == 0 else ())
                    pev = em.cnt["PE"]
                    dvv = em.op("vector", lambda e, tca=tca: e.tensor_scalar(
                        out=vsb[:, tca * 260:tca * 260 + 260].rearrange(
                            "p (kv d) -> p kv d", kv=4, d=65)[:, :, 0:64],
                        in0=pb[6][:, 0:256].rearrange("p (kv d) -> p kv d",
                                                      kv=4, d=64),
                        scalar1=pg["rscol"][:, tca:tca + 1], scalar2=None,
                        op0=ALU.mult),
                        wait=[("PE", pev), ("GP", gp_ones),
                              ("DV", pg["rscoldep"])])
                    bankdep[6] = ("DV", dvv)
                    cfg["vdv"][tca] = dvv

            prev_attn_pe = 0
            pass1_dv = em.cnt["DV"]
            for qt in range(nqt):
                q0 = qt * QT
                emit_rs16(pg, q0, QT)
                qdv_byc = [0] * NCH
                for mp in range(NCH // 2):
                    mcs = [2 * mp, 2 * mp + 1]
                    pe_done = []
                    for i, mc in enumerate(mcs):
                        waps, wwait, wpar = wload(cfg["wq"], mc, 1024,
                                                  scale_col=cfg["ncol"])
                        for k in range(NCH):
                            pemm(pb[i][:, 0:QT], waps[:, k * 128:(k + 1) * 128],
                                 xsl(k, q0, QT), k == 0, k == NCH - 1,
                                 wait=(wwait + bwait(i)) if k == 0 else ())
                        wfree(wpar)
                        pe_done.append(em.cnt["PE"])
                    dsts = [qu[:, mc * QT:(mc + 1) * QT] for mc in mcs]
                    qdvs = rope_qknorm(mcs, q0, QT, cfg["bandq"], cfg["selq"], 16,
                                       [pb[0], pb[1]], pe_done, dsts,
                                       [[("PE", prev_attn_pe), ("DV", pass1_dv)]] * 2)
                    bankdep[0] = ("DV", qdvs[0])
                    bankdep[1] = ("DV", qdvs[1])
                    qdv_byc[mcs[0]] = qdvs[0]
                    qdv_byc[mcs[1]] = qdvs[1]
                kmaxc = (q0 + QT) // 128
                kminc = 0 if win is None else max(0, (q0 - win) // 128)
                nkc = kmaxc - kminc
                for pc in range(NCH):
                    hA, hB = cfg["pairs"][pc]
                    kvA, kvB = cfg["kvmap"](hA), cfg["kvmap"](hB)
                    pdep = {}
                    pvpe = {}
                    def emit_s(ki):
                        kc = kminc + ki
                        ri = ki % 2
                        delta = kc * 128 - q0
                        pcol = None
                        if delta > -128:
                            pcol = pen_ge(delta)
                        if win is not None and delta < QT - 1 - cfg["winhi"]:
                            pcol = pen_le(delta + cfg["winhi"])
                        for hi, kv in ((0, kvA), (1, kvB)):
                            base = 64 * hi
                            bank = pb[2 + hi]
                            sv = pemm(bank[:, 0:QT],
                                      kpr[base:base + 64,
                                          (kv // 2) * S + kc * 128:
                                          (kv // 2) * S + kc * 128 + 128],
                                      qu[base:base + 64, pc * QT:pc * QT + QT],
                                      True, True,
                                      wait=bwait(2 + hi) +
                                      ([("DV", qdv_byc[pc])] if ki == 0 else []),
                                      inc=True)
                            ring_w = ([("PE", pvpe[(ki - 2, hi)])]
                                      if (ki - 2, hi) in pvpe else [])
                            po = (ri * 2 + hi) * QT
                            acx = em.op("scalar", lambda e, bank=bank, po=po, QT=QT:
                                        e.activation(
                                out=pring[:, po:po + QT], in_=bank[:, 0:QT],
                                func=AF.Exp, bias=col_neg8[:, 0:1], scale=0.125),
                                wait=[("PE", sv)] + ring_w)
                            bankdep[2 + hi] = ("AC", acx)
                            if pcol is not None:
                                gpm = em.op("gpsimd", lambda e, po=po, QT=QT,
                                            pcol=pcol: e.tensor_tensor(
                                    out=pring[:, po:po + QT],
                                    in0=pring[:, po:po + QT],
                                    in1=seli[:, pcol:pcol + QT], op=ALU.mult),
                                    wait=[("AC", acx)])
                                pdep[(ki, hi)] = [("GP", gpm)]
                            else:
                                pdep[(ki, hi)] = [("AC", acx)]

                    def emit_pv(ki):
                        kc = kminc + ki
                        ri = ki % 2
                        for hi, kv in ((0, kvA), (1, kvB)):
                            po = (ri * 2 + hi) * QT
                            for half in range(2):
                                bnk = pb[4 + 2 * hi + half]
                                pvpe[(ki, hi)] = pemm(
                                    bnk[0:65, 0:QT],
                                    vsb[64 * half:64 * half + 64,
                                        kc * 260 + kv * 65:kc * 260 + kv * 65 + 65],
                                    pring[64 * half:64 * half + 64, po:po + QT],
                                    ki == 0, ki == nkc - 1,
                                    wait=pdep[(ki, hi)] +
                                    (bwait(4 + 2 * hi + half) if ki == 0 else []),
                                    inc=True)

                    for ki in range(nkc):
                        emit_s(ki)
                        if ki >= 1:
                            emit_pv(ki - 1)
                    emit_pv(nkc - 1)
                    rowsc = pg["rows"]
                    for hi in range(2):
                        em.op("vector", lambda e, hi=hi, QT=QT: e.tensor_copy(
                            rn16[64 + 32 * hi:65 + 32 * hi, 0:QT],
                            pb[4 + 2 * hi][64:65, 0:QT]),
                            wait=[("PE", pvpe[(nkc - 1, hi)])])
                        em.op("vector", lambda e, hi=hi, QT=QT: e.tensor_tensor(
                            out=rn16[64 + 32 * hi:65 + 32 * hi, 0:QT],
                            in0=rn16[64 + 32 * hi:65 + 32 * hi, 0:QT],
                            in1=pb[5 + 2 * hi][64:65, 0:QT], op=ALU.add))
                        em.op("vector", lambda e, hi=hi, QT=QT: e.reciprocal(
                            out=rn16[64 + 32 * hi:65 + 32 * hi, 0:QT],
                            in_=rn16[64 + 32 * hi:65 + 32 * hi, 0:QT]))
                    dvr = em.cnt["DV"]
                    for hi in range(2):
                        em.op("vector", lambda e, hi=hi, pc=pc, QT=QT: e.tensor_copy(
                            ao[64 * hi:64 * hi + 64, pc * QT:pc * QT + QT],
                            pb[4 + 2 * hi][0:64, 0:QT]))
                        em.op("vector", lambda e, hi=hi, pc=pc, QT=QT:
                              e.tensor_tensor(
                            out=ao[64 * hi:64 * hi + 64, pc * QT:pc * QT + QT],
                            in0=ao[64 * hi:64 * hi + 64, pc * QT:pc * QT + QT],
                            in1=pb[5 + 2 * hi][0:64, 0:QT], op=ALU.add))
                    pevb = pemm(pb[2][:, 0:QT], selpair, rn16[:, 0:QT], True, True,
                                wait=[("DV", dvr)] + bwait(2))
                    dvo = em.op("vector", lambda e, pc=pc, QT=QT: e.tensor_tensor(
                        out=ao[:, pc * QT:pc * QT + QT],
                        in0=ao[:, pc * QT:pc * QT + QT],
                        in1=pb[2][:, 0:QT], op=ALU.mult), wait=[("PE", pevb)])
                    bankdep[2] = ("DV", dvo)
                    for i_ in range(4, 8):
                        bankdep[i_] = ("DV", dvo)
                aodep = em.cnt["DV"]
                for mc in range(NCH):
                    waps, wwait, wpar = wload(cfg["wo"], mc, 1024)
                    bk = mc % 2
                    for k in range(NCH):
                        pemm(pb[bk][:, 0:QT], waps[:, k * 128:(k + 1) * 128],
                             ao[:, k * QT:(k + 1) * QT], k == 0, k == NCH - 1,
                             wait=(wwait + bwait(bk) + [("DV", aodep)])
                             if k == 0 else ())
                    wfree(wpar)
                    pev = em.cnt["PE"]
                    dvx = em.op("vector", lambda e, mc=mc, q0=q0, QT=QT, bk=bk:
                                e.tensor_tensor(
                        out=xsl(mc, q0, QT), in0=xsl(mc, q0, QT),
                        in1=pb[bk][:, 0:QT], op=ALU.add), wait=[("PE", pev)])
                    bankdep[bk] = ("DV", dvx)
                prev_attn_pe = em.cnt["PE"]
            A.pop()
            return em.cnt["DV"]

        A.push()
        vup_pg = A.take(512)
        vup_ap = vup_pg
        vw = em.dma("sync", vup_pg[:, 0:256], dr["g_vup"][0], "LDV",
                    wait=[("LD", ld_init)])
        vw = em.dma("sync", vup_pg[:, 256:512], dr["g_vup"][1], "LDV")
        vup_wait = [("LDV", vw)]
        lwv_ap = None
        lwv_wait = None

        mla_cfg = dict(name="mla", Ct=0, St=1, window=None, QT=256,
                       wq=dr["g_wq"], wo=dr["g_wo"], wk=None, ncol=n1col,
                       lat=True, kvmap=lambda h: h // 4, pairs=PAIRS,
                       bandq=band_mla, selq=0, winhi=None,
                       kdv={}, vdv={},
                       kdst_wait=lambda tt, mc: [],
                       xdep=[("LD", ld_init), ("GP", gp_init)])
        emit_gqa(mla_cfg)
        A.pop()

        if upto == "mla":
            finish()
            return nc, em

        A.push()
        lwv_pg = A.take(NCH * 256)
        lwv_ap = lwv_pg
        lwv_w = None
        for kc in range(NCH):
            lwv_w = em.dma("sync", lwv_pg[:, kc * 256:(kc + 1) * 256],
                           dr["l_wv"][kc], "LDL",
                           wait=[("DV", em.cnt["DV"]), ("AC", em.cnt["AC"]),
                                 ("GP", em.cnt["GP"])] if kc == 0 else ())
        lwv_wait = [("LDL", lwv_w)]
        lwv_dv = None
        for kc in range(NCH):
            lwv_dv = em.op("vector", lambda e, kc=kc: e.tensor_scalar(
                out=lwv_pg[:, kc * 256:(kc + 1) * 256],
                in0=lwv_pg[:, kc * 256:(kc + 1) * 256],
                scalar1=n2col[:, kc:kc + 1], scalar2=None, op0=ALU.mult),
                wait=lwv_wait if kc == 0 else ())
        lwv_wait = [("DV", lwv_dv)]
        xdv_mla = em.cnt["DV"]
        loc_cfg = dict(name="loc", Ct=2, St=3, window=256, QT=256,
                       wq=dr["l_wq"], wo=dr["l_wo"], wk=dr["l_wk"], ncol=n2col,
                       lat=False, kvmap=lambda h: h // 4, pairs=PAIRS,
                       bandq=band_mla, selq=0, winhi=255,
                       kdv={}, vdv={},
                       kdst_wait=lambda tt, mc: [],
                       xdep=[("DV", xdv_mla), ("PE", em.cnt["PE"])])
        emit_gqa(loc_cfg)
        A.pop()

        if upto == "local":
            finish()
            return nc, em

        def emit_ln(src2):
            A.push()
            rows = A.take(1536)
            pgx = A.take(4096)
            sq = A.take(1024)
            scrm = A.take(512)
            ldx0 = {"v": None}
            src2r = src2.rearrange("(c p) t -> p c t", p=128)
            for tt in range(4):
                t0 = tt * 512
                page = pgx
                pager = page.rearrange("p (c t) -> p c t", c=NCH)
                wv = em.dma("sync", pager, src2r[:, :, t0:t0 + 512], "LDX",
                            wait=[("DV", em.cnt["DV"])])
                mml = {}
                zdv = {}
                for c in range(NCH):
                    zdv[c] = em.op("vector", lambda e, c=c, t0=t0, page=page:
                                   e.tensor_tensor(
                        out=xsl(c, t0, 512), in0=xsl(c, t0, 512),
                        in1=page[:, c * 512:(c + 1) * 512], op=ALU.add),
                        wait=[("LDX", wv), ("PE", em.cnt["PE"])] if c == 0
                        else [("LDX", wv)])
                    sl = sq[:, (c % 2) * 512:(c % 2) * 512 + 512]
                    acv = em.op("scalar", lambda e, c=c, t0=t0, sl=sl: e.activation(
                        out=sl, in_=xsl(c, t0, 512), func=AF.Square),
                        wait=[("DV", zdv[c])] +
                        ([("PE", mml[c % 2])] if c % 2 in mml else []))
                    pemm(pb[2][:], sumsel, xsl(c, t0, 512), c == 0, c == NCH - 1,
                         wait=([("DV", zdv[c])] + (bwait(2) if c == 0 else [])),
                         inc=True)
                    mml[c % 2] = pemm(pb[3][:], sumsel, sl, c == 0, c == NCH - 1,
                                      wait=[("AC", acv)] + (bwait(3) if c == 0
                                                            else []), inc=True)
                pev = em.cnt["PE"]
                murow = rows[0:1, 0:512]
                invrow = rows[0:1, 512:1024]
                tmprow = rows[0:1, 1024:1536]
                d1 = em.op("vector", lambda e: e.tensor_scalar(
                    out=murow, in0=pb[2][0:1, :], scalar1=1.0 / D,
                    scalar2=None, op0=ALU.mult), wait=[("PE", pev)])
                bankdep[2] = ("DV", d1)
                em.op("vector", lambda e: e.tensor_tensor(
                    out=tmprow, in0=murow, in1=murow, op=ALU.mult))
                d3 = em.op("vector", lambda e: e.scalar_tensor_tensor(
                    out=invrow, in0=pb[3][0:1, :], scalar=1.0 / D,
                    in1=tmprow, op0=ALU.mult, op1=ALU.subtract))
                bankdep[3] = ("DV", d3)
                a1 = em.op("scalar", lambda e: e.activation(
                    out=invrow, in_=invrow, func=AF.Sqrt,
                    bias=col_eps5[0:1, 0:1]), wait=[("DV", d3)])
                em.op("vector", lambda e: e.reciprocal(
                    out=invrow, in_=invrow), wait=[("AC", a1)])
                d5 = em.op("vector", lambda e: e.tensor_tensor(
                    out=murow, in0=murow, in1=invrow, op=ALU.mult))
                p1 = pemm(pb[4][:], onesrow, invrow, True, True,
                          wait=[("DV", d5)] + bwait(4), inc=True)
                p2 = pemm(pb[5][:], onesrow, murow, True, True,
                          wait=bwait(5), inc=True)
                for c in range(NCH):
                    em.op("vector", lambda e, c=c, t0=t0: e.tensor_tensor(
                        out=scrm, in0=xsl(c, t0, 512), in1=pb[4][:], op=ALU.mult),
                        wait=[("PE", p2)])
                    dvx = em.op("vector", lambda e, c=c, t0=t0: e.tensor_tensor(
                        out=xsl(c, t0, 512), in0=scrm, in1=pb[5][:],
                        op=ALU.subtract))
                bankdep[4] = ("DV", em.cnt["DV"])
                bankdep[5] = ("DV", em.cnt["DV"])
            A.pop()

        emit_ln(dr["xT"])
        if upto == "postln":
            finish()
            return nc, em

        x3dv = em.cnt["DV"]
        for c in range(NCH):
            em.dma("gpsimd", dr["x3sp"][c * 128:(c + 1) * 128, :], xsl(c, 0, S),
                   "XSP", wait=[("DV", x3dv)] if c == 0 else ())
        x3sp_done = em.cnt["XSP"]

        A.push()
        nrow = A.take(1536)
        phase_w = [("DV", em.cnt["DV"]), ("PE", em.cnt["PE"]), ("AC", em.cnt["AC"]),
                   ("GP", em.cnt["GP"])]
        lv = em.dma("sync", tabs[:, 0:2 * S], dr["c_xpos"][:, :], "LDV",
                    wait=phase_w)
        qrow = nrow[0:1, 0:2 * DEPTH * HD]
        qksp = nrow[0:1, 1024:1536]
        lq = em.dma("sync", qrow, dr["f_qksrow"][0:1, :], "LDV")
        for L in range(DEPTH):
            em.op("vector", lambda e, L=L: e.reduce_max(
                out=qksp[:, L:L + 1], in_=qrow[:, L * HD:(L + 1) * HD],
                axis=mybir.AxisListType.X), wait=[("LDV", lq)])
            em.op("vector", lambda e, L=L: e.reduce_max(
                out=qksp[:, DEPTH + L:DEPTH + L + 1],
                in_=qrow[:, DEPTH * HD + L * HD:DEPTH * HD + (L + 1) * HD],
                axis=mybir.AxisListType.X))
            em.op("vector", lambda e, L=L: e.tensor_tensor(
                out=qksp[:, 12 + L:13 + L], in0=qksp[:, L:L + 1],
                in1=qksp[:, DEPTH + L:DEPTH + L + 1], op=ALU.mult))
            dvc_ = em.op("vector", lambda e, L=L: e.tensor_scalar(
                out=qksp[:, 12 + L:13 + L], in0=qksp[:, 12 + L:13 + L],
                scalar1=-8.0, scalar2=None, op0=ALU.mult))
            pc_ = pemm(pb[7][:, L:L + 1], onesrow, qksp[:, 12 + L:13 + L],
                       True, True, wait=[("DV", dvc_), ("GP", gp_init2)] +
                       (bwait(7) if L == 0 else []), inc=True)
        dvnc = em.op("vector", lambda e: e.tensor_copy(csm[:, 40:40 + DEPTH],
                                                       pb[7][:, 0:DEPTH]),
                     wait=[("PE", em.cnt["PE"])])
        bankdep[7] = ("DV", dvnc)
        xpos_ld = em.cnt["LDV"]
        A.pop()

        if upto == "xpos":
            finish()
            return nc, em

        A.push()
        rn16F = A.take(512)
        dcolA = A.take(16)
        dcolB = A.take(16)
        ov0 = A.off
        kwin = A.take(8 * 1024)
        vring = A.take(6 * 1040)
        quF = A.take(8 * 512)
        aoF = A.take(8 * 256)
        sqF = A.take(1024)
        pringF = sqF
        szA = A.off - ov0
        A.off = ov0
        hS = A.take(NH2 * 512)
        xnF = A.take(8 * 512)
        gtF = A.take(1024)
        rowsF = A.take(1536)
        lncolp = A.take(16)
        sqB = A.take(1024)
        A.off = ov0 + max(szA, A.off - ov0)
        em.op("gpsimd", lambda e: e.memset(rn16F, 0.0),
              wait=[("DV", em.cnt["DV"]), ("PE", em.cnt["PE"])])
        gpF0 = em.cnt["GP"]
        BT = [(0, 512), (512, 512), (1024, 512), (1536, 512)]

        def ffn_layer(L):
            for sl_ in range(6):
                em.op("gpsimd", lambda e, sl_=sl_: e.memset(
                    vring[:, sl_ * 1040 + 64:(sl_ + 1) * 1040:65], 1.0),
                    wait=[("DV", em.cnt["DV"]), ("PE", em.cnt["PE"])]
                    if sl_ == 0 else ())
            gpF = em.cnt["GP"]
            qscol = qkc[:, L:L + 1]
            kscol = qkc[:, DEPTH + L:DEPTH + L + 1]
            negc = csm[:, 40 + L:41 + L]

            prev_proj_pe = {"v": 0}
            dcol_pending = {"v": None}
            for st in range(4):
                t0 = st * 512
                xblkdv = em.cnt["DV"]
                prev_proj_pe["v"] = em.cnt["PE"]
                def proj_qk(wdram, scol, At, Bt, SLt, dsts):
                    for mp in range(4):
                        mcs = [2 * mp, 2 * mp + 1]
                        pe_raw = []
                        for i, mc in enumerate(mcs):
                            waps, wwait, wpar = wload(wdram, mc, 1024)
                            for k in range(NCH):
                                pemm(pb[i][:], waps[:, k * 128:(k + 1) * 128],
                                     xsl(k, t0, 512, shift=(k >= 4)),
                                     k == 0, k == NCH - 1,
                                     wait=(wwait + bwait(i) +
                                           [("DV", xblkdv)]) if k == 0 else ())
                            wfree(wpar)
                            pe_raw.append(em.cnt["PE"])
                        for i, mc in enumerate(mcs):
                            acs = em.op("scalar", lambda e, i=i: e.activation(
                                out=sqF[:, 512:1024], in_=pb[i][:],
                                func=AF.Square),
                                wait=[("PE", pe_raw[i]), ("PE", em.cnt["PE"])])
                            pemm(pb[4][:], band_ffn(mc), sqF[:, 512:1024],
                                 mc == 0, mc == NCH - 1,
                                 wait=[("AC", acs)] + (bwait(4) if mc == 0
                                                       else []), inc=True)
                            dvz = em.op("vector", lambda e, i=i, scol=scol:
                                        e.tensor_scalar(
                                out=pb[2 + i][:], in0=pb[i][:],
                                scalar1=scol, scalar2=None, op0=ALU.mult),
                                wait=[("AC", acs)] + bwait(2 + i))
                            bankdep[i] = ("DV", dvz)
                            d = dsts[mc]
                            em.op("vector", lambda e, i=i, d=d, t0=t0, At=At:
                                  e.tensor_tensor(
                                out=d, in0=pb[2 + i][:],
                                in1=tabs[:, At * S + t0:At * S + t0 + 512],
                                op=ALU.mult),
                                wait=[("PE", prev_proj_pe["v"])])
                            for qd in range(4):
                                pq, ps = qd * 32, (qd ^ 1) * 32
                                em.op("vector", lambda e, i=i, pq=pq, ps=ps,
                                      t0=t0, Bt=Bt: e.tensor_tensor(
                                    out=sqF[pq:pq + 32, 0:512],
                                    in0=pb[2 + i][ps:ps + 32, :],
                                    in1=tabs[pq:pq + 32,
                                             Bt * S + t0:Bt * S + t0 + 512],
                                    op=ALU.mult))
                            dvu = em.op("vector", lambda e, d=d: e.tensor_tensor(
                                out=d, in0=d, in1=sqF[:, 0:512], op=ALU.add))
                            bankdep[2 + i] = ("DV", dvu)
                            em.op("vector", lambda e, d=d, t0=t0, SLt=SLt:
                                  e.tensor_tensor(
                                out=d, in0=d,
                                in1=tabs[:, SLt * S + t0:SLt * S + t0 + 512],
                                op=ALU.mult), wait=[("LDV", xpos_ld)])
                    pev = em.cnt["PE"]
                    a1 = em.op("scalar", lambda e: e.activation(
                        out=rn16F[0:16, 0:512], in_=pb[4][0:16, :],
                        func=AF.Sqrt), wait=[("PE", pev), ("GP", gpF),
                                             ("DV", em.cnt["DV"])])
                    bankdep[4] = ("AC", a1)
                    em.op("vector", lambda e: e.tensor_scalar(
                        out=rn16F[0:16, 0:512], in0=rn16F[0:16, 0:512],
                        scalar1=1e-12, scalar2=None, op0=ALU.max),
                        wait=[("AC", a1)])
                    d3 = em.op("vector", lambda e: e.reciprocal(
                        out=rn16F[0:16, 0:512], in_=rn16F[0:16, 0:512]))
                    for mc in range(NCH):
                        pevb = pemm(pb[5][:],
                                    selc[0:16, (8 + mc) * 128:(9 + mc) * 128],
                                    rn16F[0:16, 0:512], True, True,
                                    wait=[("DV", d3)] + bwait(5),
                                    inc=True)
                        dvm = em.op("vector", lambda e, d=dsts[mc]:
                                    e.tensor_tensor(out=d, in0=d,
                                                    in1=pb[5][:],
                                                    op=ALU.mult),
                                    wait=[("PE", pevb)])
                        bankdep[5] = ("DV", dvm)

                qdsts = [quF[:, mc * 512:(mc + 1) * 512] for mc in range(NCH)]
                proj_qk(dr["f_wq"][L], qscol, 2, 3, 0, qdsts)
                slot0 = (2 * st) % 4
                kdsts = [kwin[:, mc * 1024 + slot0 * 256:
                              mc * 1024 + slot0 * 256 + 512]
                         for mc in range(NCH)]
                proj_qk(dr["f_wk"][L], kscol, 2, 3, 1, kdsts)
                for kc in range(NCH):
                    waps, wwait, wpar = wload(dr["f_wv"][L], kc, 1024)
                    for tc in range(4):
                        for nd in range(2):
                            pemm(pb[2 * tc + nd][:],
                                 xsl(kc, t0 + tc * 128, 128, shift=(kc >= 4)),
                                 waps[:, nd * 512:(nd + 1) * 512],
                                 kc == 0, kc == NCH - 1,
                                 wait=(wwait + bwait(2 * tc + nd) +
                                       [("DV", em.cnt["DV"])])
                                 if kc == 0 else wwait, inc=True)
                    wfree(wpar)
                pev = em.cnt["PE"]
                for tc in range(4):
                    tg = (4 * st + tc) % 6
                    for nd in range(2):
                        dvv = em.op("vector", lambda e, tg=tg, nd=nd, tc=tc:
                                    e.tensor_copy(
                            vring[:, tg * 1040:(tg + 1) * 1040].rearrange(
                                "p (kv d) -> p kv d", kv=16, d=65)
                            [:, nd * 8:(nd + 1) * 8, 0:64],
                            pb[2 * tc + nd][:].rearrange(
                                "p (kv d) -> p kv d", kv=8, d=64)),
                            wait=[("PE", pev), ("GP", gpF)])
                        bankdep[2 * tc + nd] = ("DV", dvv)
                if dcol_pending["v"] is not None:
                    pdv, pcol = dcol_pending["v"]
                    for c in range(4, NCH):
                        em.op("vector", lambda e, c=c, pcol=pcol: e.tensor_tensor(
                            out=xsl(c, pcol, 1), in0=xsl(c, pcol, 1),
                            in1=dcolA[:, c:c + 1], op=ALU.add),
                            wait=[("PE", em.cnt["PE"]), ("DV", pdv)])
                    dcol_pending["v"] = None
                blkdv = em.cnt["DV"]
                for b2 in range(2):
                    blk = 2 * st + b2
                    bt0 = blk * 256
                    kminc = max(0, (bt0 - 256) // 128)
                    kmaxc = (bt0 + 256) // 128
                    nkc = kmaxc - kminc
                    for pc in range(NCH):
                        pdep = {}
                        pvpe = {}

                        def emit_s(ki):
                            kc = kminc + ki
                            ri = ki % 2
                            delta = kc * 128 - bt0
                            kblk = kc // 2
                            pcol = None
                            if delta > -128:
                                pcol = pen_ge(delta)
                            if delta < -1:
                                pcol = pen_le(delta + 256)
                            for hi in range(2):
                                base = 64 * hi
                                bank = pb[2 + hi]
                                kap = kwin[base:base + 64,
                                           pc * 1024 + (kblk % 4) * 256 +
                                           (kc % 2) * 128:
                                           pc * 1024 + (kblk % 4) * 256 +
                                           (kc % 2) * 128 + 128]
                                sv = pemm(bank[:, 0:256], kap,
                                          quF[base:base + 64,
                                              pc * 512 + b2 * 256:
                                              pc * 512 + b2 * 256 + 256],
                                          True, True,
                                          wait=bwait(2 + hi) +
                                          ([("DV", blkdv)] if ki == 0 else []),
                                          inc=True)
                                po = (ri * 2 + hi) * 256
                                acx = em.op("scalar", lambda e, bank=bank, po=po:
                                            e.activation(
                                    out=pringF[:, po:po + 256],
                                    in_=bank[:, 0:256],
                                    func=AF.Exp, bias=negc[:, 0:1], scale=8.0),
                                    wait=[("PE", sv)] +
                                    ([("PE", pvpe[(ki - 2, hi)])]
                                     if (ki - 2, hi) in pvpe else []))
                                bankdep[2 + hi] = ("AC", acx)
                                if pcol is not None:
                                    gpm = em.op("gpsimd", lambda e, po=po,
                                                pcol=pcol: e.tensor_tensor(
                                        out=pringF[:, po:po + 256],
                                        in0=pringF[:, po:po + 256],
                                        in1=seli[:, pcol:pcol + 256],
                                        op=ALU.mult),
                                        wait=[("AC", acx)])
                                    pdep[(ki, hi)] = [("GP", gpm)]
                                else:
                                    pdep[(ki, hi)] = [("AC", acx)]

                        def emit_pv(ki):
                            kc = kminc + ki
                            ri = ki % 2
                            for hi in range(2):
                                kv = 2 * pc + hi
                                po = (ri * 2 + hi) * 256
                                vbase = (kc % 6) * 1040 + kv * 65
                                for half in range(2):
                                    bnk = pb[4 + 2 * hi + half]
                                    pvpe[(ki, hi)] = pemm(
                                        bnk[0:65, 0:256],
                                        vring[64 * half:64 * half + 64,
                                              vbase:vbase + 65],
                                        pringF[64 * half:64 * half + 64,
                                               po:po + 256],
                                        ki == 0, ki == nkc - 1,
                                        wait=pdep[(ki, hi)] +
                                        (bwait(4 + 2 * hi + half) if ki == 0
                                         else []), inc=True)

                        for ki in range(nkc):
                            emit_s(ki)
                            if ki >= 1:
                                emit_pv(ki - 1)
                        emit_pv(nkc - 1)
                        for hi in range(2):
                            em.op("vector", lambda e, hi=hi: e.tensor_copy(
                                rn16F[64 + 32 * hi:65 + 32 * hi, 256:512],
                                pb[4 + 2 * hi][64:65, 0:256]),
                                wait=[("PE", pvpe[(nkc - 1, hi)])])
                            em.op("vector", lambda e, hi=hi: e.tensor_tensor(
                                out=rn16F[64 + 32 * hi:65 + 32 * hi, 256:512],
                                in0=rn16F[64 + 32 * hi:65 + 32 * hi, 256:512],
                                in1=pb[5 + 2 * hi][64:65, 0:256], op=ALU.add))
                            em.op("vector", lambda e, hi=hi: e.reciprocal(
                                out=rn16F[64 + 32 * hi:65 + 32 * hi, 256:512],
                                in_=rn16F[64 + 32 * hi:65 + 32 * hi, 256:512]))
                        dvr = em.cnt["DV"]
                        for hi in range(2):
                            em.op("vector", lambda e, hi=hi, pc=pc:
                                  e.tensor_copy(
                                aoF[64 * hi:64 * hi + 64,
                                    pc * 256:pc * 256 + 256],
                                pb[4 + 2 * hi][0:64, 0:256]))
                            em.op("vector", lambda e, hi=hi, pc=pc:
                                  e.tensor_tensor(
                                out=aoF[64 * hi:64 * hi + 64,
                                        pc * 256:pc * 256 + 256],
                                in0=aoF[64 * hi:64 * hi + 64,
                                        pc * 256:pc * 256 + 256],
                                in1=pb[5 + 2 * hi][0:64, 0:256], op=ALU.add))
                        pevb = pemm(pb[2][:, 0:256], selpair,
                                    rn16F[:, 256:512], True, True,
                                    wait=[("DV", dvr)] + bwait(2), inc=True)
                        dvo = em.op("vector", lambda e, pc=pc: e.tensor_tensor(
                            out=aoF[:, pc * 256:pc * 256 + 256],
                            in0=aoF[:, pc * 256:pc * 256 + 256],
                            in1=pb[2][:, 0:256], op=ALU.mult),
                            wait=[("PE", pevb)])
                        bankdep[2] = ("DV", dvo)
                        for i_ in range(4, 8):
                            bankdep[i_] = ("DV", dvo)
                    aodep = em.cnt["DV"]
                    for mc in range(NCH):
                        waps, wwait, wpar = wload(dr["f_wo"][L], mc, 1024)
                        bk = mc % 2
                        for k in range(NCH):
                            pemm(pb[bk][:, 0:256],
                                 waps[:, k * 128:(k + 1) * 128],
                                 aoF[:, k * 256:(k + 1) * 256], k == 0,
                                 k == NCH - 1,
                                 wait=(wwait + bwait(bk) + [("DV", aodep)])
                                 if k == 0 else ())
                        wfree(wpar)
                        pev = em.cnt["PE"]
                        defer = (mc >= 4 and b2 == 1 and st < 3)
                        wdt = 255 if defer else 256
                        dvx = em.op("vector", lambda e, mc=mc, bt0=bt0, bk=bk,
                                    wdt=wdt: e.tensor_tensor(
                            out=xsl(mc, bt0, wdt), in0=xsl(mc, bt0, wdt),
                            in1=pb[bk][:, 0:wdt], op=ALU.add),
                            wait=[("PE", pev), ("XSP", x3sp_done)])
                        if defer:
                            dvx = em.op("vector", lambda e, mc=mc, bk=bk:
                                        e.tensor_copy(dcolA[:, mc:mc + 1],
                                                      pb[bk][:, 255:256]))
                        bankdep[bk] = ("DV", dvx)
                    if b2 == 1 and st < 3:
                        dcol_pending["v"] = (em.cnt["DV"], bt0 + 255)

            if upto == "ffnattn0" and L == 0:
                return
            lnw = em.dma("sync", lncolp, dr["f_lnrow"][L], "LDX",
                         wait=[("PE", em.cnt["PE"]), ("DV", em.cnt["DV"])])
            b1w = em.dma("sync", b1c[:, :], dr["f_b1"][L], "LDX")
            lnw = b1w
            dcolB_pending = {"v": None}
            nst = len(BT)
            for st, (T0, W) in enumerate(BT):
                mml = {}
                for c in range(NCH):
                    sl = sqB[:, (c % 2) * 512:(c % 2) * 512 + W]
                    acv = em.op("scalar", lambda e, c=c, T0=T0, W=W, sl=sl:
                                e.activation(
                        out=sl, in_=xsl(c, T0, W, shift=(c >= 4)),
                        func=AF.Square),
                        wait=[("DV", em.cnt["DV"]), ("PE", em.cnt["PE"])]
                        if c == 0 else
                        ([("PE", mml[c % 2])] if c % 2 in mml else []))
                    pemm(pb[2][:, 0:W], sumsel, xsl(c, T0, W, shift=(c >= 4)),
                         c == 0, c == NCH - 1,
                         wait=(bwait(2) + [("DV", em.cnt["DV"])]) if c == 0
                         else [], inc=True)
                    mml[c % 2] = pemm(pb[3][:, 0:W], sumsel, sl, c == 0,
                                      c == NCH - 1,
                                      wait=[("AC", acv)] + (bwait(3) if c == 0
                                                            else []), inc=True)
                pev = em.cnt["PE"]
                murow = rowsF[0:1, 0:512]
                invrow = rowsF[0:1, 512:1024]
                tmprow = rowsF[0:1, 1024:1536]
                d1 = em.op("vector", lambda e, W=W: e.tensor_scalar(
                    out=murow[:, 0:W], in0=pb[2][0:1, 0:W], scalar1=1.0 / D,
                    scalar2=None, op0=ALU.mult), wait=[("PE", pev)])
                bankdep[2] = ("DV", d1)
                em.op("vector", lambda e, W=W: e.tensor_tensor(
                    out=tmprow[:, 0:W], in0=murow[:, 0:W], in1=murow[:, 0:W],
                    op=ALU.mult))
                d3 = em.op("vector", lambda e, W=W: e.scalar_tensor_tensor(
                    out=invrow[:, 0:W], in0=pb[3][0:1, 0:W], scalar=1.0 / D,
                    in1=tmprow[:, 0:W], op0=ALU.mult, op1=ALU.subtract))
                bankdep[3] = ("DV", d3)
                a1 = em.op("scalar", lambda e, W=W: e.activation(
                    out=invrow[:, 0:W], in_=invrow[:, 0:W], func=AF.Sqrt,
                    bias=col_eps5[0:1, 0:1]), wait=[("DV", d3)])
                d4 = em.op("vector", lambda e, W=W: e.reciprocal(
                    out=invrow[:, 0:W], in_=invrow[:, 0:W]), wait=[("AC", a1)])
                p1 = pemm(pb[4][:, 0:W], onesrow, murow[:, 0:W], True, True,
                          wait=[("DV", d4)] + bwait(4), inc=True)
                p2 = pemm(pb[5][:, 0:W], onesrow, invrow[:, 0:W], True, True,
                          wait=bwait(5), inc=True)
                for c in range(NCH):
                    em.op("vector", lambda e, c=c, T0=T0, W=W: e.tensor_tensor(
                        out=xnF[:, c * 512:c * 512 + W],
                        in0=xsl(c, T0, W, shift=(c >= 4)), in1=pb[4][:, 0:W],
                        op=ALU.subtract), wait=[("PE", p2)])
                    em.op("vector", lambda e, c=c, W=W: e.tensor_tensor(
                        out=xnF[:, c * 512:c * 512 + W],
                        in0=xnF[:, c * 512:c * 512 + W], in1=pb[5][:, 0:W],
                        op=ALU.mult))
                    dxn = em.op("vector", lambda e, c=c, W=W: e.tensor_scalar(
                        out=xnF[:, c * 512:c * 512 + W],
                        in0=xnF[:, c * 512:c * 512 + W],
                        scalar1=lncolp[:, c:c + 1],
                        scalar2=lncolp[:, 8 + c:9 + c],
                        op0=ALU.mult, op1=ALU.add),
                        wait=[("LDX", lnw)])
                    bankdep[4] = ("DV", dxn)
                    bankdep[5] = ("DV", dxn)
                xndv = em.cnt["DV"]
                if dcolB_pending["v"] is not None:
                    pdv, pcol = dcolB_pending["v"]
                    for c in range(4, NCH):
                        em.op("vector", lambda e, c=c, pcol=pcol:
                              e.tensor_tensor(
                            out=xsl(c, pcol, 1), in0=xsl(c, pcol, 1),
                            in1=dcolB[:, c:c + 1], op=ALU.add),
                            wait=[("DV", pdv)])
                    dcolB_pending["v"] = None
                for cc in range(NH2):
                    wapsA, wwaitA, wparA = wload(dr["f_w1"][L][cc, :, 0:1024],
                                                 slice(None), 1024)
                    wapsG, wwaitG, wparG = wload(dr["f_w1"][L][cc, :, 1024:2048],
                                                 slice(None), 1024)
                    bi = cc % 2
                    for k in range(NCH):
                        pemm(pb[0 + bi][:, 0:W], wapsA[:, k * 128:(k + 1) * 128],
                             xnF[:, k * 512:k * 512 + W], k == 0, k == NCH - 1,
                             wait=(wwaitA + bwait(bi) + [("DV", xndv)])
                             if k == 0 else ())
                    wfree(wparA)
                    pea = em.cnt["PE"]
                    for k in range(NCH):
                        pemm(pb[2 + bi][:, 0:W],
                             wapsG[:, k * 128:(k + 1) * 128],
                             xnF[:, k * 512:k * 512 + W], k == 0, k == NCH - 1,
                             wait=(wwaitG + bwait(2 + bi)) if k == 0 else ())
                    wfree(wparG)
                    peg = em.cnt["PE"]
                    acg = em.op("scalar", lambda e, bi=bi, cc=cc, W=W:
                                e.activation(
                        out=gtF[:, bi * 512:bi * 512 + W],
                        in_=pb[2 + bi][:, 0:W],
                        func=AF.Gelu, bias=b1c[:, NH2 + cc:NH2 + cc + 1]),
                        wait=[("PE", peg), ("LDX", b1w)])
                    bankdep[2 + bi] = ("AC", acg)
                    dvh = em.op("vector", lambda e, bi=bi, cc=cc, W=W:
                                e.scalar_tensor_tensor(
                        out=hS[:, cc * 512:cc * 512 + W],
                        in0=pb[0 + bi][:, 0:W],
                        scalar=b1c[:, cc:cc + 1],
                        in1=gtF[:, bi * 512:bi * 512 + W],
                        op0=ALU.add, op1=ALU.mult),
                        wait=[("PE", pea), ("AC", acg)])
                    bankdep[0 + bi] = ("DV", dvh)
                hdv = em.cnt["DV"]
                W2P = ((0, 1024), (1024, 2048), (2048, DIP))
                for m in range(NCH):
                    parts = [wload(dr["f_w2"][L][m, :, 0:1024], slice(None),
                                   1024),
                             wload(dr["f_w2"][L][m, :, 1024:2048], slice(None),
                                   1024), None]
                    bk = 4 + (m % 2)
                    for k in range(NH2):
                        pi = min(k // 8, 2)
                        waps_, wwait_, wpar_ = parts[pi]
                        koff = k - 8 * pi
                        pemm(pb[bk][:, 0:W],
                             waps_[:, koff * 128:(koff + 1) * 128],
                             hS[:, k * 512:k * 512 + W], k == 0, k == NH2 - 1,
                             wait=((wwait_ + bwait(bk) + [("DV", hdv)])
                                   if k == 0 else
                                   (wwait_ if k in (8, 16) else ())),
                             inc=True)
                        if k == 7:
                            wfree(parts[0][2])
                            parts[2] = wload(dr["f_w2"][L][m, :, 2048:DIP],
                                             slice(None), DIP - 2048)
                        if k == 15:
                            wfree(parts[1][2])
                    wfree(parts[2][2])
                    pev = em.cnt["PE"]
                    defer = (m >= 4 and st < nst - 1)
                    wdt = W - 1 if defer else W
                    dvx = em.op("vector", lambda e, m=m, T0=T0, bk=bk, wdt=wdt:
                                e.scalar_tensor_tensor(
                        out=xsl(m, T0, wdt), in0=pb[bk][:, 0:wdt],
                        scalar=b2c[:, L * 8 + m:L * 8 + m + 1],
                        in1=xsl(m, T0, wdt), op0=ALU.add, op1=ALU.add),
                        wait=[("PE", pev)])
                    if defer:
                        dvx = em.op("vector", lambda e, m=m, bk=bk, L=L, W=W:
                                    e.tensor_scalar(
                            out=dcolB[:, m:m + 1], in0=pb[bk][:, W - 1:W],
                            scalar1=b2c[:, L * 8 + m:L * 8 + m + 1],
                            scalar2=None, op0=ALU.add))
                    bankdep[bk] = ("DV", dvx)
                if st < nst - 1:
                    dcolB_pending["v"] = (em.cnt["DV"], T0 + W - 1)

        for L in range(DEPTH):
            ffn_layer(L)
            if upto == ("ffn%d" % L) or upto == "ffnattn0":
                break

        if upto is not None and upto.startswith("ffn"):
            finish()
            return nc, em
        A.pop()

        emit_ln(dr["x3sp"])

        finish()
        return nc, em

def kernel(**inputs):
    consts = host_consts()
    wts = host_weights(inputs)
    hid = np.asarray(inputs["L0_attn_hidden"])
    nc, _ = build_program()
    in_maps = []
    for b in range(B):
        m = {"xT": np.ascontiguousarray(hid[b].T)}
        m.update(consts)
        m.update(wts)
        in_maps.append(m)
    res = run_bass_kernel_spmd(nc, in_maps, core_ids=list(range(B)))
    out = np.stack([np.ascontiguousarray(r["out"].T) for r in res.results])
    return out
